# version 07
# baseline (speedup 1.0000x reference)
import numpy as np
from contextlib import ExitStack

import concourse.bass as bass
import concourse.mybir as mybir
from concourse.bass_utils import run_bass_kernel_spmd

F32 = mybir.dt.float32
I32 = mybir.dt.int32
AF = mybir.ActivationFunctionType
ALU = mybir.AluOpType
AXX = mybir.AxisListType.X

C64 = 1.5 * 2.0 ** 52
C32 = 1.5 * 2.0 ** 23

D = 1024
DFF = 4096
LQ = 2048
LK = 4096
NTI = 16
NCC = 8
NFC = 32
NT4 = 4
NQC = 8
NKC = 32
HLOC = 8
H = 16
DH = 64
EPS = 1e-5
NBAND = 5
HSZ = DH * LQ

S_RB7 = 16
S_EPS = 17
S_SQ = 18
S_MEAN = 19
S_E2 = 20
S_VAR = 21
S_RSTD = 22
S_NEGMB = 23
S_AR = 24
S_REC = 25
S_SX = 26
S_SH = 27
S_AMY = 28
S_AMH = 29
S_WA = 30
S_SF = 31
S_GAM = 32
S_T1 = 33
S_T2 = 34
S_NCB = 35
S_NC0 = 36
S_NC31 = 37
S_RSTD2 = 38
S_GAM2 = 39


class Sem:
    def __init__(self, h):
        self.h = h
        self.v = 0

    def bump(self, n=1):
        self.v += n
        return self.v


class Build:
    def __init__(self, taps=()):
        nc = bass.Bass("TRN2", target_bir_lowering=False, detect_race_conditions=True)
        self.nc = nc
        self.cmd = {"t": [], "v": [], "s": [], "g": [], "y": []}
        self.regs = {}
        d = nc.dram_tensor
        self.mid = d("mid", [1, 1], I32, kind="ExternalInput")
        self.xin = d("xin", [LQ, D], F32, kind="ExternalInput")
        self.wqT = d("wqT", [D, D], F32, kind="ExternalInput")
        self.wkT = d("wkT", [D, D], F32, kind="ExternalInput")
        self.wvT = d("wvT", [D, D], F32, kind="ExternalInput")
        self.woT = d("woT", [D, D], F32, kind="ExternalInput")
        self.bqd = d("bqd", [D], F32, kind="ExternalInput")
        self.bkd = d("bkd", [D], F32, kind="ExternalInput")
        self.bvd = d("bvd", [D], F32, kind="ExternalInput")
        self.bod = d("bod", [D], F32, kind="ExternalInput")
        self.WinT = d("WinT", [DFF, D], F32, kind="ExternalInput")
        self.bind = d("bind", [DFF], F32, kind="ExternalInput")
        self.Wout = d("Wout", [DFF, D], F32, kind="ExternalInput")
        self.boutd = d("boutd", [D], F32, kind="ExternalInput")
        self.rbT = d("rbT", [H, 32], F32, kind="ExternalInput")
        self.btab = d("btab", [H, NBAND, 128, 512], F32, kind="ExternalInput")
        self.out = d("out", [LQ, D], F32, kind="ExternalOutput")
        self.hid = d("hid", [NFC * 128 * LQ], F32)
        self.nsp = d("nsp", [NFC * 128 * D], F32)
        self.agin = d("agin", [3 * H * HSZ], F32)
        self.agout = d("agout", [8, 3, H, HSZ], F32)
        self.ag2in = d("ag2in", [HLOC * DH * LK], F32)
        self.ag2out = d("ag2out", [8, HLOC, DH, LK], F32)
        self.arin = d("arin", [4, 1], F32)
        self.arout = d("arout", [4, 1], F32)
        self.msk = d("msk", [128, 2], F32, kind="ExternalInput")
        self.agl = d("agl", [2, 3, H, HSZ], F32)
        self.ag2l = d("ag2l", [2, HLOC, DH, LK], F32)
        self.taps = dict(taps)
        self.tapd = {}
        for name, shape in self.taps.items():
            self.tapd[name] = d(name, shape, F32, kind="ExternalOutput")

    def emit(self, e, fn, sem=None, inc=1):
        if e == "v":
            own = self.s_v
        elif e == "s":
            own = self.s_s
        else:
            own = None
        if own is not None:
            assert sem is None or sem is own, "v/s ops must milestone their own sem"
            if own.v > 0:
                self.cmd[e].append(lambda E, s=own.h, vv=own.v: E.wait_ge(s, vv))
            own.bump(1)
            self.cmd[e].append(lambda E, fn=fn, s=own.h: fn(E).then_inc(s, 1))
            return
        if sem is not None:
            sem.bump(inc)
            self.cmd[e].append(lambda E, fn=fn, s=sem.h, i=inc: fn(E).then_inc(s, i))
        else:
            self.cmd[e].append(fn)

    def wait(self, e, sem, val=None):
        v = sem.v if val is None else val
        if v > 0:
            self.cmd[e].append(lambda E, s=sem.h, vv=v: E.wait_ge(s, vv))

    def barrier(self):
        b = self.s_bar
        for e in self.cmd:
            self.wait(e, self.s_g)
            self.wait(e, self.s_y)
            self.wait(e, self.s_cc)
            self.wait(e, self.s_t)
            self.wait(e, self.s_v)
            self.wait(e, self.s_s)
            self.wait(e, self.s_gp)
            self.wait(e, self.s_ar)
            for s in self.s_w + self.s_h2 + self.s_p + self.s_gq + self.s_yq:
                self.wait(e, s)
        for _ in range(5):
            b.v += 1
        tgt = b.v
        for e in self.cmd:
            self.cmd[e].append(lambda E, s=b.h: E.nop().then_inc(s, 1))
        for e in self.cmd:
            self.cmd[e].append(lambda E, s=b.h, vv=tgt: E.wait_ge(s, vv))

    def dma(self, q, fn):
        pool = self.s_gq if q == "g" else self.s_yq
        par = self.qpar[q]
        self.qpar[q] ^= 1
        s = pool[par]
        self.wait(q, s)
        self.emit(q, fn, s, 16)
        return (s, s.v)

    def waitd(self, e, tok):
        if tok is not None:
            self.wait(e, tok[0], tok[1])

    def qfence(self, e):
        for s in self.s_gq + self.s_yq:
            self.wait(e, s)

    def col(self, i):
        return self.sm[:, i:i + 1]

    def tap(self, name, ap_fn):
        if name not in self.tapd:
            return
        t = self.tapd[name]
        self.barrier()
        self.dma("g", lambda E, o=t, f=ap_fn: E.dma_start(o.ap(), f()))
        self.barrier()

    def build(self):
        nc = self.nc
        with ExitStack() as stk:
            sem = lambda n: Sem(stk.enter_context(nc.semaphore(n)))
            sb = lambda n, shape, dt=F32: stk.enter_context(nc.sbuf_tensor(n, shape, dt))
            self.s_y = sem("s_y")
            self.s_g = sem("s_g")
            self.s_cc = sem("s_cc")
            self.s_t = sem("s_t")
            self.s_v = sem("s_v")
            self.s_s = sem("s_s")
            self.s_gp = sem("s_gp")
            self.s_bar = sem("s_bar")
            self.s_ar = sem("s_ar")
            self.s_gq = [sem("s_gq0"), sem("s_gq1")]
            self.s_yq = [sem("s_yq0"), sem("s_yq1")]
            self.qpar = {"g": 0, "y": 0}
            self.s_w = [sem("s_w0"), sem("s_w1")]
            self.s_h2 = [sem("s_h0"), sem("s_h1")]
            self.s_p = [sem("s_p0"), sem("s_p1")]
            self.x = sb("x", [128, NTI * D])
            self.ar = sb("arn", [128, 16384])
            self.r1 = sb("r1", [128, 2048])
            self.r2 = sb("r2", [128, 2048])
            self.r3 = sb("r3", [128, 2048])
            self.big = sb("big", [128, 8192])
            self.ones = sb("ones1", [128, 128])
            self.ident = sb("ident", [128, 128])
            self.io_r = sb("io_r", [128, 128])
            self.binv = sb("binv", [128, 32])
            self.bovh = sb("bovh", [128, 8])
            self.bqv = sb("bqv", [128, 8])
            self.bkv = sb("bkv", [128, 8])
            self.bvv = sb("bvv", [128, 8])
            self.bo_v = sb("bo_v", [128, 8])
            self.srv = sb("srv", [128, 32])
            self.gmv = sb("gmv", [128, 32])
            self.nq2 = sb("nq2", [128, 16])
            self.nk2 = sb("nk2", [128, 16])
            self.nq8 = sb("nq8", [128, 8])
            self.nk8 = sb("nk8", [128, 8])
            self.mskv = sb("mskv", [128, 2])
            self.sm = sb("sm", [128, 64])
            self.rbh = sb("rbh", [1, 64])
            self.a32 = sb("a32", [128, 1024])
            self.zb = sb("zb", [128, 512])
            self.ustg = sb("ustg", [128, 512])
            self.idt = sb("idt", [1, 1], I32)
            self.pb = [stk.enter_context(nc.psum_tensor("pb%d" % i, [128, 512], F32))
                       for i in range(8)]

            self.emit_all()

            blk = stk.enter_context(nc.Block())

            @blk.tensor
            def _(E):
                for f in self.cmd["t"]:
                    f(E)

            @blk.vector
            def _(E):
                for f in self.cmd["v"]:
                    f(E)

            @blk.scalar
            def _(E):
                for f in self.cmd["s"]:
                    f(E)

            @blk.gpsimd
            def _(E):
                for f in self.cmd["g"]:
                    f(E)

            @blk.sync
            def _(E):
                for f in self.cmd["y"]:
                    f(E)
        return nc

    def emit_all(self):
        s_y, s_g, s_v, s_t, s_s = self.s_y, self.s_g, self.s_v, self.s_t, self.s_s
        for ti in range(NTI):
            self.dma("g", lambda E, ti=ti: E.dma_start(
                self.x[:, D * ti:D * (ti + 1)], self.xin[128 * ti:128 * (ti + 1), :]))
        self.dma("g", lambda E: E.dma_start(
            self.binv[:, :], bass.AP(self.bind, 0, [[1, 128], [128, 32]]),
            allow_slow_non_contiguous=True))
        for vec, dst in [(self.boutd, self.bovh), (self.bqd, self.bqv),
                         (self.bkd, self.bkv), (self.bvd, self.bvv),
                         (self.bod, self.bo_v)]:
            self.dma("g", lambda E, v=vec, dd=dst: E.dma_start(
                dd[:, :], bass.AP(v, 0, [[1, 128], [128, 8]]),
                allow_slow_non_contiguous=True))
        self.dma("g", lambda E: E.dma_start(self.ar[0:16, 0:32], self.rbT[:, :]))
        idtok = self.dma("g", lambda E: E.dma_start(self.idt[:, :], self.mid[:, :]))
        self.dma("g", lambda E: E.dma_start(self.mskv[:, :], self.msk[:, :]))
        self.emit("g", lambda E: E.memset(self.ones[:, :], 1.0))
        self.emit("g", lambda E: E.memset(self.col(S_EPS), EPS))
        self.emit("g", lambda E: E.iota(
            bass.AP(self.io_r, 0, [[128, 128], [1, 128]]), [[1, 128]],
            channel_multiplier=0, allow_small_or_imprecise_dtypes=True))
        self.emit("g", lambda E: E.iota(
            bass.AP(self.ident, 0, [[128, 128], [1, 128]]), [[0, 128]],
            channel_multiplier=1, allow_small_or_imprecise_dtypes=True), self.s_gp)

        self.qfence("g")
        _idsem, _idval = idtok

        def mkregs(E):
            E.wait_ge(_idsem.h, _idval)
            rk = E.alloc_register(name="rk")
            E.load(rk, self.idt[0:1, 0:1])
            jj = E.alloc_register(name="jj")
            E.reg_mod(jj, rk, 2)
            r0 = E.alloc_register(name="r0")
            E.reg_sub(r0, rk, jj)
            self.regs.update(rk=rk, jj=jj, r0=r0)
        self.cmd["g"].append(mkregs)

        self.qfence("v")
        self.wait("v", self.s_gp)
        self.emit("v", lambda E: E.tensor_tensor(
            self.ident[:, :], self.ident[:, :], self.io_r[:, :], ALU.is_equal))
        self.emit("v", lambda E: E.tensor_scalar(
            self.bovh[:, :], self.bovh[:, :], 0.5, None, ALU.mult))
        self.emit("v", lambda E: E.tensor_scalar(
            self.bkv[:, :], self.bkv[:, :], 0.125, None, ALU.mult))
        self.emit("v", lambda E: E.tensor_reduce(
            self.ar[0:16, 32:33], self.ar[0:16, 0:32], axis=AXX, op=ALU.max), s_v)
        self.wait("t", s_v)
        self.emit("t", lambda E: E.transpose(
            self.pb[7][0:1, 0:16], self.ar[0:16, 32:33], self.ident[0:16, 0:16]), s_t)
        self.wait("v", s_t)
        self.emit("v", lambda E: E.tensor_reduce(
            self.ar[0:1, 33:34], self.pb[7][0:1, 0:16], axis=AXX, op=ALU.max), s_v)
        self.wait("t", s_v)
        self.emit("t", lambda E: E.matmul(
            self.pb[7][:, 0:1], self.ones[0:1, 0:128], self.ar[0:1, 33:34],
            start=True, stop=True), s_t)
        self.wait("v", s_t)
        self.emit("v", lambda E: E.tensor_scalar(
            self.col(S_RB7), self.pb[7][:, 0:1], -1.0, 7.0, ALU.mult, ALU.add), s_v)
        self.barrier()

        self.tap("tap_x0", lambda: self.x[:, 0:16384])
        self.ffn(0)
        self.attn()
        self.ffn(1)

        self.barrier()
        for ti in range(NTI):
            self.dma("y", lambda E, ti=ti: E.dma_start(
                self.out[128 * ti:128 * (ti + 1), :], self.x[:, D * ti:D * (ti + 1)]))
        self.qfence("y")

    def layernorm_to_arena(self, amax=False):
        s_v, s_t, s_s = self.s_v, self.s_t, self.s_s
        if amax:
            self.emit("g", lambda E: E.memset(self.col(S_AMY), 0.0), self.s_gp)
            self.wait("v", self.s_gp)
        guard = {}
        tprev = [0]
        for ti in range(NTI):
            xt = self.x[:, D * ti:D * (ti + 1)]
            self.wait("v", s_t, tprev[0])
            self.emit("v", lambda E, xt=xt: E.tensor_reduce(
                self.col(S_MEAN), xt, axis=AXX, op=ALU.add))
            self.emit("v", lambda E: E.tensor_scalar(
                self.col(S_MEAN), self.col(S_MEAN), 1.0 / D, None, ALU.mult), s_v)
            self.wait("s", s_v)
            self.emit("s", lambda E, xt=xt: E.activation(
                self.r1[:, 0:D], xt, AF.Square), s_s)
            self.wait("v", s_s)
            self.emit("v", lambda E: E.tensor_reduce(
                self.col(S_E2), self.r1[:, 0:D], axis=AXX, op=ALU.add))
            self.emit("v", lambda E: E.tensor_scalar(
                self.col(S_E2), self.col(S_E2), 1.0 / D, None, ALU.mult))
            self.emit("v", lambda E: E.tensor_scalar(
                self.col(S_VAR), self.col(S_MEAN), self.col(S_MEAN), None, ALU.mult))
            self.emit("v", lambda E: E.tensor_tensor(
                self.col(S_VAR), self.col(S_E2), self.col(S_VAR), ALU.subtract), s_v)
            self.wait("s", s_v)
            self.emit("s", lambda E: E.activation(
                self.col(S_RSTD), self.col(S_VAR), AF.Sqrt, bias=self.col(S_EPS)), s_s)
            self.wait("v", s_s)
            self.emit("v", lambda E: E.reciprocal(self.col(S_RSTD2), self.col(S_RSTD)))
            self.emit("v", lambda E: E.tensor_scalar(
                self.col(S_NEGMB), self.col(S_MEAN), self.col(S_RSTD2), -1.0,
                ALU.mult, ALU.mult))
            self.emit("v", lambda E, xt=xt: E.tensor_scalar(
                self.r2[:, 0:D], xt, self.col(S_RSTD2), self.col(S_NEGMB),
                ALU.mult, ALU.add), s_v)
            if amax:
                self.emit("v", lambda E: E.tensor_reduce(
                    self.col(S_SQ), self.r2[:, 0:D], axis=AXX, op=ALU.max,
                    apply_absolute_value=True))
                self.emit("v", lambda E: E.tensor_tensor(
                    self.col(S_AMY), self.col(S_AMY), self.col(S_SQ), ALU.max), s_v)
            yv = s_v.v
            for cc in range(NCC):
                bi = 4 + (cc % 2)
                self.wait("t", s_v, max(yv, guard.get(bi, 0)))
                self.emit("t", lambda E, bi=bi, cc=cc: E.transpose(
                    self.pb[bi][:, 0:128], self.r2[:, 128 * cc:128 * (cc + 1)],
                    self.ident[:, :]), s_t)
                self.wait("v", s_t)
                dst = self.ar[:, 2048 * cc + 128 * ti: 2048 * cc + 128 * (ti + 1)]
                self.emit("v", lambda E, bi=bi, dd=dst: E.tensor_copy(
                    dd, self.pb[bi][:, 0:128]), s_v)
                guard[bi] = s_v.v
            tprev[0] = s_t.v

    def fold_allreduce(self, amax_col, slot, out_col):
        s_v, s_t, s_g = self.s_v, self.s_t, self.s_g
        self.barrier()
        self.emit("t", lambda E, c=amax_col: E.transpose(
            self.pb[7][0:1, 0:128], self.col(c), self.ident[:, :]), s_t)
        self.wait("v", s_t)
        self.emit("v", lambda E: E.tensor_reduce(
            self.sm[0:1, S_AR:S_AR + 1], self.pb[7][0:1, 0:128], axis=AXX,
            op=ALU.max))
        self.emit("v", lambda E: E.tensor_scalar(
            self.sm[0:1, S_AR:S_AR + 1], self.sm[0:1, S_AR:S_AR + 1], 1e-8, None,
            ALU.max), s_v)
        self.wait("g", s_v)
        self.emit("g", lambda E, sl=slot: E.dma_start(
            self.arin[sl:sl + 1, :], self.sm[0:1, S_AR:S_AR + 1]), self.s_ar, 16)
        self.wait("g", self.s_ar)
        self.emit("g", lambda E, sl=slot: E.collective_compute(
            "AllReduce", ALU.max, replica_groups=[list(range(8))],
            ins=[self.arin[sl:sl + 1, :]], outs=[self.arout[sl:sl + 1, :]]),
            self.s_cc)
        self.wait("g", self.s_cc)
        self.emit("g", lambda E, sl=slot: E.dma_start(
            self.sm[0:1, S_AR:S_AR + 1], self.arout[sl:sl + 1, :]), self.s_ar, 16)
        self.wait("t", self.s_ar)
        self.emit("t", lambda E: E.matmul(
            self.pb[7][:, 0:1], self.ones[0:1, 0:128], self.sm[0:1, S_AR:S_AR + 1],
            start=True, stop=True), s_t)
        self.wait("v", s_t)
        self.emit("v", lambda E: E.reciprocal(self.col(S_REC), self.pb[7][:, 0:1]))
        self.emit("v", lambda E, oc=out_col: E.tensor_scalar(
            self.col(oc), self.col(S_REC), 32767.0, None, ALU.mult), s_v)
        self.barrier()

    def round_chain(self, src, dst, scol, s0, width):
        u = self.r1[:, 0:width]
        q = self.r3[:, 0:width]
        self.emit("v", lambda E, s=src, u=u, sc=scol, s0=s0: E.tensor_scalar(
            u, s, sc, s0, ALU.mult, ALU.add))
        self.emit("v", lambda E, u=u, q=q: E.tensor_scalar(
            q, u, C64, -C64, ALU.add, ALU.add))
        self.emit("v", lambda E, u=u, q=q: E.tensor_tensor(u, u, q, ALU.subtract))
        self.emit("v", lambda E, u=u: E.tensor_scalar(
            u, u, C32, -C32, ALU.add, ALU.add))
        self.emit("v", lambda E, u=u: E.scalar_tensor_tensor(
            u, u, 1.0, u, ALU.is_ge, ALU.mult))
        self.emit("v", lambda E, q=q, u=u, dd=dst, s0=s0: E.scalar_tensor_tensor(
            dd, q, -s0, u, ALU.add, ALU.add), self.s_v)

    def ffn(self, idx):
        s_v, s_t, s_s, s_g, s_y = self.s_v, self.s_t, self.s_s, self.s_g, self.s_y
        self.layernorm_to_arena(amax=True)
        self.barrier()
        self.tap("tap_yT%d" % idx, lambda: self.ar[:, 0:16384])
        self.fold_allreduce(S_AMY, 2 * idx, S_SX)
        for cc in range(NCC):
            a = self.ar[:, 2048 * cc:2048 * (cc + 1)]
            self.round_chain(a, a, self.col(S_SX), 40000.0, 2048)
        self.barrier()
        self.tap("tap_M%d" % idx, lambda: self.ar[:, 0:16384])

        self.emit("g", lambda E: E.memset(self.col(S_AMH), 0.0), self.s_gp)
        self.wait("v", self.s_gp)
        wsl = [self.big[:, 0:1024], self.big[:, 1024:2048]]
        ntl = [self.big[:, 2048:3072], self.big[:, 3072:4096]]
        hsl = [self.big[:, 4096:6144], self.big[:, 6144:8192]]
        w_free = [0, 0]
        h_free = [0, 0]
        bank_free = [0, 0, 0, 0]
        tguard = {}
        for ff in range(NFC):
            pp = ff % 2
            self.wait("y", s_t, w_free[pp])
            self.emit("y", lambda E, ff=ff, pp=pp: E.dma_start(
                wsl[pp], self.WinT[128 * ff:128 * (ff + 1), :]), self.s_w[pp], 16)
            self.wait("v", self.s_w[pp])
            w = wsl[pp]
            self.emit("v", lambda E, w=w: E.tensor_reduce(
                self.col(S_WA), w, axis=AXX, op=ALU.max, apply_absolute_value=True))
            self.emit("v", lambda E: E.tensor_scalar(
                self.col(S_WA), self.col(S_WA), 1e-8, None, ALU.max))
            self.emit("v", lambda E: E.reciprocal(self.col(S_SF), self.col(S_WA)))
            self.emit("v", lambda E: E.tensor_scalar(
                self.col(S_SF), self.col(S_SF), 127.0, None, ALU.mult))
            self.emit("v", lambda E: E.tensor_scalar(
                self.col(S_GAM2), self.col(S_SF), self.col(S_SX), None, ALU.mult))
            self.emit("v", lambda E: E.reciprocal(self.col(S_GAM), self.col(S_GAM2)),
                      s_v)
            self.round_chain(w, w, self.col(S_SF), 384.0, 1024)
            nq = s_v.v
            for c8 in range(NCC):
                bi = 4 + (c8 % 2)
                self.wait("t", s_v, max(nq, tguard.get(bi, 0)))
                self.emit("t", lambda E, bi=bi, w=w, c8=c8: E.transpose(
                    self.pb[bi][:, 0:128], w[:, 128 * c8:128 * (c8 + 1)],
                    self.ident[:, :]), s_t)
                self.wait("v", s_t)
                self.emit("v", lambda E, bi=bi, c8=c8, pp=pp: E.tensor_copy(
                    ntl[pp][:, 128 * c8:128 * (c8 + 1)], self.pb[bi][:, 0:128]), s_v)
                tguard[bi] = s_v.v
            w_free[pp] = s_t.v
            ntv = s_v.v
            self.wait("t", s_v, ntv)
            for t4 in range(NT4):
                self.wait("t", s_s, bank_free[t4])
            for c8 in range(NCC):
                for t4 in range(NT4):
                    last = (c8 == NCC - 1 and t4 == NT4 - 1)
                    self.emit("t", lambda E, c8=c8, t4=t4, pp=pp: E.matmul(
                        self.pb[t4][:, :], ntl[pp][:, 128 * c8:128 * (c8 + 1)],
                        self.ar[:, 2048 * c8 + 512 * t4: 2048 * c8 + 512 * (t4 + 1)],
                        start=(c8 == 0), stop=(c8 == NCC - 1)),
                        s_t if last else None)
            mmv = s_t.v
            self.wait("s", s_t, mmv)
            self.wait("s", s_v, s_v.v)
            self.wait("s", self.s_p[pp], h_free[pp])
            for t4 in range(NT4):
                self.emit("s", lambda E, t4=t4, pp=pp, ff=ff: E.activation(
                    hsl[pp][:, 512 * t4:512 * (t4 + 1)], self.pb[t4][:, :], AF.Gelu,
                    bias=self.binv[:, ff:ff + 1], scale=self.col(S_GAM)), s_s)
                bank_free[t4] = s_s.v
            sv = s_s.v
            self.wait("v", s_s, sv)
            self.emit("v", lambda E, pp=pp: E.tensor_reduce(
                self.col(S_T1), hsl[pp], axis=AXX, op=ALU.max,
                apply_absolute_value=True))
            self.emit("v", lambda E: E.tensor_tensor(
                self.col(S_AMH), self.col(S_AMH), self.col(S_T1), ALU.max), s_v)
            self.wait("g", s_s, sv)
            self.emit("g", lambda E, ff=ff, pp=pp: E.dma_start(
                bass.AP(self.hid, ff * 128 * LQ, [[LQ, 128], [1, LQ]]), hsl[pp]),
                self.s_p[pp], 16)
            h_free[pp] = self.s_p[pp].v
        self.barrier()

        self.fold_allreduce(S_AMH, 2 * idx + 1, S_SH)
        wq_free = [None, None]
        for rr in range(NFC):
            pp = rr % 2
            self.waitd("y", wq_free[pp])
            self.emit("y", lambda E, rr=rr, pp=pp: E.dma_start(
                wsl[pp], self.Wout[128 * rr:128 * (rr + 1), :]), self.s_w[pp], 16)
            self.wait("v", self.s_w[pp])
            w = wsl[pp]
            self.emit("v", lambda E, w=w: E.tensor_reduce(
                self.col(S_WA), w, axis=AXX, op=ALU.max, apply_absolute_value=True))
            self.emit("v", lambda E: E.tensor_scalar(
                self.col(S_WA), self.col(S_WA), 1e-8, None, ALU.max))
            self.emit("v", lambda E: E.reciprocal(self.col(S_SF), self.col(S_WA)))
            self.emit("v", lambda E, rr=rr: E.tensor_scalar(
                self.srv[:, rr:rr + 1], self.col(S_SF), 127.0, None, ALU.mult), s_v)
            self.round_chain(w, w, self.srv[:, rr:rr + 1], 384.0, 1024)
            self.wait("g", s_v)
            wq_free[pp] = self.dma("g", lambda E, rr=rr, w=w: E.dma_start(
                bass.AP(self.nsp, rr * 128 * D, [[D, 128], [1, D]]), w))
        self.barrier()
        self.emit("v", lambda E: E.tensor_scalar(
            self.nq2[:, 0:16], self.srv[:, 0:16], self.col(S_SH), None, ALU.mult))
        self.emit("v", lambda E: E.tensor_scalar(
            self.nk2[:, 0:16], self.srv[:, 16:32], self.col(S_SH), None, ALU.mult))
        self.emit("v", lambda E: E.reciprocal(self.gmv[:, 0:16], self.nq2[:, 0:16]))
        self.emit("v", lambda E: E.reciprocal(self.gmv[:, 16:32], self.nk2[:, 0:16]),
                  s_v)
        self.barrier()

        nsl = [self.big[:, 0:1024], self.big[:, 1024:2048]]
        htl = [self.big[:, 2048 + 512 * i:2048 + 512 * (i + 1)] for i in range(2)]
        esl = self.big[:, 4096:8192]
        n_free = [0, 0]
        ht_free = [0, 0]
        m2g, m2gv = 0, 0
        for t4 in range(NT4):
            for rr in range(NFC):
                pp = rr % 2
                self.wait("y", s_t, n_free[pp])
                self.emit("y", lambda E, rr=rr, pp=pp: E.dma_start(
                    nsl[pp], bass.AP(self.nsp, rr * 128 * D, [[D, 128], [1, D]])),
                    self.s_w[pp], 16)
                nready = self.s_w[pp].v
                self.wait("g", s_t, ht_free[pp])
                self.emit("g", lambda E, rr=rr, t4=t4, pp=pp: E.dma_start(
                    htl[pp], bass.AP(self.hid, rr * 128 * LQ + 512 * t4,
                                     [[LQ, 128], [1, 512]])), self.s_h2[pp], 16)
                self.wait("v", self.s_h2[pp])
                self.round_chain(htl[pp], htl[pp], self.col(S_SH), 40000.0, 512)
                self.emit("v", lambda E, pp=pp, rr=rr: E.tensor_scalar(
                    htl[pp], htl[pp], self.gmv[:, rr:rr + 1], None, ALU.mult), s_v)
                self.wait("t", s_v)
                self.wait("t", self.s_w[pp], nready)
                if rr == 0:
                    self.wait("t", s_s, m2g)
                    self.wait("t", s_v, m2gv)
                for o8 in range(8):
                    last = (o8 == 7)
                    self.emit("t", lambda E, o8=o8, pp=pp, rr=rr: E.matmul(
                        self.pb[o8][:, :], nsl[pp][:, 128 * o8:128 * (o8 + 1)],
                        htl[pp], start=(rr == 0), stop=(rr == NFC - 1)),
                        s_t if last else None)
                n_free[pp] = s_t.v
                ht_free[pp] = s_t.v
            mm2 = s_t.v
            self.wait("s", s_t, mm2)
            for o8 in range(8):
                self.emit("s", lambda E, o8=o8: E.activation(
                    esl[:, 512 * o8:512 * (o8 + 1)], self.pb[o8][:, :], AF.Identity,
                    bias=self.bovh[:, o8:o8 + 1], scale=0.5), s_s)
            m2g = s_s.v
            ev = s_s.v
            guard = {}
            for o8 in range(8):
                for j4 in range(4):
                    bi = o8 % 2
                    self.wait("t", s_s, ev)
                    self.wait("t", s_v, guard.get(bi, 0))
                    self.emit("t", lambda E, bi=bi, o8=o8, j4=j4: E.transpose(
                        self.pb[bi][:, 0:128],
                        esl[:, 512 * o8 + 128 * j4: 512 * o8 + 128 * (j4 + 1)],
                        self.ident[:, :]), s_t)
                    self.wait("v", s_t)
                    ti = 4 * t4 + j4
                    dst = self.x[:, D * ti + 128 * o8: D * ti + 128 * (o8 + 1)]
                    self.emit("v", lambda E, bi=bi, dd=dst: E.tensor_tensor(
                        dd, dd, self.pb[bi][:, 0:128], ALU.add), s_v)
                    guard[bi] = s_v.v
            m2gv = s_v.v
        self.barrier()
        self.tap("tap_xf%d" % idx, lambda: self.x[:, 0:16384])

    def attn(self):
        s_v, s_t, s_s, s_g, s_y = self.s_v, self.s_t, self.s_s, self.s_g, self.s_y
        self.layernorm_to_arena(amax=False)
        self.barrier()
        self.tap("tap_yTa", lambda: self.ar[:, 0:16384])
        self.emit("g", lambda E: E.memset(self.nq2[:, :], 0.0))
        self.emit("g", lambda E: E.memset(self.nk2[:, :], 0.0), self.s_gp)
        self.wait("v", self.s_gp)
        self.wait("t", self.s_gp)
        wfull = self.big
        for wt, bias, kind in [(self.wqT, self.bqv, "q"), (self.wkT, self.bkv, "k"),
                               (self.wvT, self.bvv, "v")]:
            self.barrier()
            for c8 in range(NCC):
                self.dma("y", lambda E, wt=wt, c8=c8: E.dma_start(
                    wfull[:, 1024 * c8:1024 * (c8 + 1)],
                    wt[128 * c8:128 * (c8 + 1), :]))
            self.qfence("t")
            bank_guard = {}
            spl_guard = {}
            for d8 in range(NCC):
                for t4 in range(NT4):
                    self.wait("t", s_v, bank_guard.get(t4, 0))
                    for c8 in range(NCC):
                        last = (c8 == NCC - 1)
                        self.emit("t", lambda E, d8=d8, t4=t4, c8=c8, last=last:
                                  E.matmul(
                            self.pb[t4][:, :],
                            wfull[:, 1024 * c8 + 128 * d8: 1024 * c8 + 128 * (d8 + 1)],
                            self.ar[:, 2048 * c8 + 512 * t4: 2048 * c8 + 512 * (t4 + 1)],
                            start=(c8 == 0), stop=last), s_t if last else None)
                    mv = s_t.v
                    self.wait("v", s_t, mv)
                    for tk in spl_guard.get(t4, []):
                        self.waitd("v", tk)
                    self.wait("v", s_s, s_s.v)
                    sc = 0.125 if kind == "k" else 1.0
                    self.emit("v", lambda E, t4=t4, d8=d8, b=bias, sc=sc: E.tensor_scalar(
                        self.r1[:, 512 * t4:512 * (t4 + 1)], self.pb[t4][:, :],
                        sc, b[:, d8:d8 + 1], ALU.mult, ALU.add), s_v)
                    bank_guard[t4] = s_v.v
                    ev = s_v.v
                    if kind in ("q", "k"):
                        self.wait("s", s_v, ev)
                        self.emit("s", lambda E, t4=t4: E.activation(
                            self.r2[:, 512 * t4:512 * (t4 + 1)],
                            self.r1[:, 512 * t4:512 * (t4 + 1)], AF.Square), s_s)
                        self.wait("t", s_s)
                        for hh in range(2):
                            self.wait("t", s_v, bank_guard.get(5, 0))
                            self.emit("t", lambda E, hh=hh, t4=t4: E.matmul(
                                self.pb[5][:, :],
                                self.ones[64 * hh:64 * (hh + 1), 0:128],
                                self.r2[64 * hh:64 * (hh + 1),
                                        512 * t4:512 * (t4 + 1)],
                                start=True, stop=True), s_t)
                            self.wait("v", s_t)
                            dst = self.nq2 if kind == "q" else self.nk2
                            hcol = 2 * d8 + hh
                            self.emit("v", lambda E: E.tensor_reduce(
                                self.col(S_T2), self.pb[5][:, :], axis=AXX,
                                op=ALU.max))
                            self.emit("v", lambda E, dd=dst, hcol=hcol: E.tensor_tensor(
                                dd[:, hcol:hcol + 1], dd[:, hcol:hcol + 1],
                                self.col(S_T2), ALU.max), s_v)
                            bank_guard[5] = s_v.v
                        self.wait("g", s_v, ev)
                        kk = 0 if kind == "k" else 1
                        toks = []
                        for hh in range(2):
                            h = 2 * d8 + hh
                            off = kk * H * HSZ + h * HSZ + 512 * t4
                            toks.append(self.dma(
                                "g", lambda E, off=off, hh=hh, t4=t4: E.dma_start(
                                    bass.AP(self.agin, off, [[LQ, 64], [1, 512]]),
                                    self.r1[64 * hh:64 * (hh + 1),
                                            512 * t4:512 * (t4 + 1)])))
                        spl_guard[t4] = toks
                    else:
                        for j4 in range(4):
                            tc = 4 * t4 + j4
                            slot = tc % 16
                            self.wait("t", s_v, max(ev, bank_guard.get(6, 0)))
                            self.emit("t", lambda E, t4=t4, j4=j4: E.transpose(
                                self.pb[6][:, 0:128],
                                self.r1[:, 512 * t4 + 128 * j4:
                                        512 * t4 + 128 * (j4 + 1)],
                                self.ident[:, :]), s_t)
                            self.wait("v", s_t)
                            for tk in spl_guard.get(300 + slot, []):
                                self.waitd("v", tk)
                            self.emit("v", lambda E, sl=slot: E.tensor_copy(
                                self.r3[:, 128 * sl:128 * (sl + 1)],
                                self.pb[6][:, 0:128]), s_v)
                            bank_guard[6] = s_v.v
                            self.wait("g", s_v)
                            toks = []
                            for hh in range(2):
                                h = 2 * d8 + hh
                                off = 2 * H * HSZ + h * HSZ + 64 * tc
                                toks.append(self.dma(
                                    "g", lambda E, off=off, sl=slot, hh=hh:
                                    E.dma_start(
                                        bass.AP(self.agin, off,
                                                [[1024, 128], [1, 64]]),
                                        self.r3[:, 128 * sl + 64 * hh:
                                                128 * sl + 64 * (hh + 1)])))
                            spl_guard[300 + slot] = toks
        self.barrier()
        self.emit("g", lambda E: E.collective_compute(
            "AllGather", ALU.bypass, replica_groups=[list(range(8))],
            ins=[self.agin.ap().opt()], outs=[self.agout.ap().opt()]), self.s_cc)
        self.wait("g", self.s_cc)

        rst_sems = []
        for kk in range(3):
            par = self.qpar["g"]
            self.qpar["g"] ^= 1
            s = self.s_gq[par]
            wv = s.v
            s.bump(16)
            rst_sems.append((s, wv, kk))

        def restage(E):
            r0s = E.snap(self.regs["r0"], min_val=0, max_val=6)
            for s, wv, kk in rst_sems:
                if wv > 0:
                    E.wait_ge(s.h, wv)
                E.dma_start(self.agl[:, kk:kk + 1, :, :],
                            self.agout[bass.ds(r0s, 2), kk:kk + 1, :, :]
                            ).then_inc(s.h, 16)
        self.cmd["g"].append(restage)
        self.emit("v", lambda E: E.tensor_scalar(
            self.nq8[:, :], self.nq2[:, 0:8], self.mskv[:, 0:1], None, ALU.mult))
        self.emit("v", lambda E: E.scalar_tensor_tensor(
            self.nq8[:, :], self.nq2[:, 8:16], self.mskv[:, 1:2], self.nq8[:, :],
            ALU.mult, ALU.add))
        self.emit("v", lambda E: E.tensor_scalar(
            self.nk8[:, :], self.nk2[:, 0:8], self.mskv[:, 0:1], None, ALU.mult))
        self.emit("v", lambda E: E.scalar_tensor_tensor(
            self.nk8[:, :], self.nk2[:, 8:16], self.mskv[:, 1:2], self.nk8[:, :],
            ALU.mult, ALU.add), s_v)
        self.barrier()

        self._expg = {}
        for hl in range(HLOC):
            self.attn_head(hl)

        self.barrier()
        self.emit("g", lambda E: E.collective_compute(
            "AllGather", ALU.bypass, replica_groups=[list(range(8))],
            ins=[self.ag2in.ap().opt()], outs=[self.ag2out.ap().opt()]), self.s_cc)
        self.wait("g", self.s_cc)

        par2 = self.qpar["g"]
        self.qpar["g"] ^= 1
        s2 = self.s_gq[par2]
        _w2 = s2.v
        s2.bump(16)

        def restage2(E):
            r0s = E.snap(self.regs["r0"], min_val=0, max_val=6)
            if _w2 > 0:
                E.wait_ge(s2.h, _w2)
            E.dma_start(self.ag2l[:, :, :, :],
                        self.ag2out[bass.ds(r0s, 2), :, :, :]
                        ).then_inc(s2.h, 16)
        self.cmd["g"].append(restage2)
        self.tap("tap_ag2l", lambda: self.ag2l.ap())
        self.barrier()
        self.emit_wo()
        self.barrier()
        self.tap("tap_xa", lambda: self.x[:, 0:16384])

    def attn_head(self, hl):
        s_v, s_t, s_s, s_g, s_y = self.s_v, self.s_t, self.s_s, self.s_g, self.s_y
        self.barrier()
        kap = self.ar[0:64, 0:4096]
        qap = self.ar[0:64, 4096:8192]

        m0 = self.mskv[:, 0:1]
        m1 = self.mskv[:, 1:2]
        jobs = []
        ag, bt = self.agl, self.btab
        for half in range(2):
            jobs.append((self.ar[0:64, 2048 * half:2048 * (half + 1)],
                         self.r1[0:64, 0:2048],
                         lambda E, h=half, hl=hl: ag[h:h + 1, 0:1, hl:hl + 1, :]
                         .squeeze(0).squeeze(0).squeeze(0)
                         .rearrange("(a b) -> a b", a=64),
                         lambda E, h=half, hl=hl: ag[h:h + 1, 0:1, 8 + hl:9 + hl, :]
                         .squeeze(0).squeeze(0).squeeze(0)
                         .rearrange("(a b) -> a b", a=64), 64))
        for half in range(2):
            jobs.append((self.ar[0:64, 4096 + 2048 * half:4096 + 2048 * (half + 1)],
                         self.r1[0:64, 0:2048],
                         lambda E, h=half, hl=hl: ag[h:h + 1, 1:2, hl:hl + 1, :]
                         .squeeze(0).squeeze(0).squeeze(0)
                         .rearrange("(a b) -> a b", a=64),
                         lambda E, h=half, hl=hl: ag[h:h + 1, 1:2, 8 + hl:9 + hl, :]
                         .squeeze(0).squeeze(0).squeeze(0)
                         .rearrange("(a b) -> a b", a=64), 64))
        for half in range(2):
            jobs.append((self.ar[:, 8192 + 1024 * half:8192 + 1024 * (half + 1)],
                         self.r1[:, 0:1024],
                         lambda E, h=half, hl=hl: ag[h:h + 1, 2:3, hl:hl + 1, :]
                         .squeeze(0).squeeze(0).squeeze(0)
                         .rearrange("(a b) -> a b", a=128),
                         lambda E, h=half, hl=hl: ag[h:h + 1, 2:3, 8 + hl:9 + hl, :]
                         .squeeze(0).squeeze(0).squeeze(0)
                         .rearrange("(a b) -> a b", a=128), 128))
        jobs.append((self.ar[:, 10240:12288], self.r2[:, 0:2048],
                     lambda E, hl=hl: bt[hl:hl + 1, :, :, :].squeeze(0)
                     .rearrange("s p q -> p (s q)")[:, 0:2048] if False else None,
                     None, 128))
        ldg = [0]

        def band_src(gidx, s0, s1):
            a = bt[gidx:gidx + 1, s0:s1, :, :].squeeze(0)
            return a.transpose([1, 0, 2])
        jobs = jobs[:-1]

        vguard = s_v.v
        for dst, c1buf, f0, f1, rows in jobs:
            self.wait("g", s_v, vguard)
            t0 = self.dma("g", lambda E, d=dst, f=f0: E.dma_start(d, f(E)))
            t1 = self.dma("g", lambda E, c=c1buf, f=f1: E.dma_start(c, f(E)))
            self.waitd("v", t0)
            self.waitd("v", t1)
            mm0 = m0[0:rows, :]
            mm1 = m1[0:rows, :]
            self.emit("v", lambda E, d=dst, mm0=mm0: E.tensor_scalar(
                d, d, mm0, None, ALU.mult))
            self.emit("v", lambda E, d=dst, c=c1buf, mm1=mm1: E.scalar_tensor_tensor(
                d, c, mm1, d, ALU.mult, ALU.add), s_v)
            vguard = s_v.v
        for s0, s1, c1buf in ((0, 4, self.r2[:, 0:2048]),
                              (4, 5, self.big[:, 2560:3072])):
            ns = s1 - s0
            dst3 = bass.AP(self.ar, 10240 + 512 * s0,
                           [[16384, 128], [512, ns], [1, 512]])
            c1b3 = (bass.AP(self.r2, 0, [[2048, 128], [512, ns], [1, 512]])
                    if s0 == 0 else
                    bass.AP(self.big, 2560, [[8192, 128], [512, ns], [1, 512]]))
            self.wait("g", s_v, vguard)
            t0 = self.dma("g", lambda E, d=dst3, s0=s0, s1=s1, hl=hl: E.dma_start(
                d, band_src(hl, s0, s1)))
            t1 = self.dma("g", lambda E, c=c1b3, s0=s0, s1=s1, hl=hl: E.dma_start(
                c, band_src(8 + hl, s0, s1)))
            self.waitd("v", t0)
            self.waitd("v", t1)
            dst = self.ar[:, 10240 + 512 * s0:10240 + 512 * s1]
            c1buf2 = c1buf[:, 0:512 * ns] if s0 == 0 else c1buf
            self.emit("v", lambda E, d=dst: E.tensor_scalar(
                d, d, m0, None, ALU.mult))
            self.emit("v", lambda E, d=dst, c=c1buf2: E.scalar_tensor_tensor(
                d, c, m1, d, ALU.mult, ALU.add), s_v)
            vguard = s_v.v
        t0 = self.dma("g", lambda E, hl=hl: E.dma_start(
            self.rbh[0:1, 0:32], self.rbT[hl:hl + 1, :]))
        t1 = self.dma("g", lambda E, hl=hl: E.dma_start(
            self.rbh[0:1, 32:64], self.rbT[8 + hl:9 + hl, :]))
        self.waitd("v", t0)
        self.waitd("v", t1)
        self.emit("v", lambda E: E.tensor_scalar(
            self.rbh[0:1, 0:32], self.rbh[0:1, 0:32], m0[0:1, :], None, ALU.mult))
        self.emit("v", lambda E: E.scalar_tensor_tensor(
            self.rbh[0:1, 0:32], self.rbh[0:1, 32:64], m1[0:1, :],
            self.rbh[0:1, 0:32], ALU.mult, ALU.add), s_v)
        self.wait("g", s_v)
        self.emit("g", lambda E: E.memset(
            bass.AP(self.big, 64, [[8192, 128], [65, 32], [1, 1]]), 1.0), self.s_gp)
        self.emit("v", lambda E: E.tensor_copy(
            bass.AP(self.big, 0, [[8192, 128], [65, 32], [1, 64]]),
            bass.AP(self.ar, 8192, [[16384, 128], [64, 32], [1, 64]])), s_v)
        self.emit("v", lambda E, hl=hl: E.tensor_scalar(
            self.col(S_T1), self.nq8[:, hl:hl + 1], self.nk8[:, hl:hl + 1], None,
            ALU.mult), s_v)
        self.wait("s", s_v)
        self.emit("s", lambda E: E.activation(
            self.col(S_T1), self.col(S_T1), AF.Sqrt), s_s)
        self.wait("v", s_s)
        self.emit("v", lambda E: E.scalar_tensor_tensor(
            self.col(S_NCB), self.col(S_T1), -1.0, self.col(S_RB7), ALU.mult,
            ALU.add), s_v)
        self.wait("t", s_v)
        self.emit("t", lambda E: E.matmul(
            self.pb[7][:, 0:2], self.ones[0:1, 0:128],
            bass.AP(self.rbh, 0, [[64, 1], [31, 2]]), start=True, stop=True), s_t)
        self.wait("v", s_t)
        self.emit("v", lambda E: E.tensor_tensor(
            self.col(S_NC0), self.col(S_NCB), self.pb[7][:, 0:1], ALU.add))
        self.emit("v", lambda E: E.tensor_tensor(
            self.col(S_NC31), self.col(S_NCB), self.pb[7][:, 1:2], ALU.add), s_v)
        prep = s_v.v
        self.wait("t", s_v, prep)
        self.wait("t", self.s_gp)
        self.wait("s", s_v, prep)

        aguard = [0, 0]
        for tq in range(NQC):
            for kc in range(NKC):
                sb_ = kc % 2
                asl = kc % 2
                self.wait("t", s_s, self._expg.get(sb_, 0))
                self.emit("t", lambda E, kc=kc, tq=tq, sb_=sb_: E.matmul(
                    self.pb[sb_][:, :], kap[:, 128 * kc:128 * (kc + 1)],
                    qap[:, 512 * tq:512 * (tq + 1)], start=True, stop=True), s_t)
                smm = s_t.v
                lo, hi = 4 * tq - 1, 4 * tq + 3
                if lo <= kc <= hi:
                    slot = kc - lo
                    if tq == 0:
                        slot = kc + 1
                    self.wait("v", s_t, smm)
                    self.emit("v", lambda E, sb_=sb_, sl=slot: E.tensor_tensor(
                        self.pb[sb_][:, :], self.pb[sb_][:, :],
                        self.ar[:, 10240 + 512 * sl:10240 + 512 * (sl + 1)],
                        ALU.add), s_v)
                    bias_col = self.col(S_NCB)
                    self.wait("s", s_v)
                elif kc < lo:
                    bias_col = self.col(S_NC31)
                    self.wait("s", s_t, smm)
                else:
                    bias_col = self.col(S_NC0)
                    self.wait("s", s_t, smm)
                self.wait("s", s_t, aguard[asl])
                self.emit("s", lambda E, sb_=sb_, asl=asl, bc=bias_col: E.activation(
                    self.a32[:, 512 * asl:512 * (asl + 1)], self.pb[sb_][:, :],
                    AF.Exp, bias=bc), s_s)
                self._expg[sb_] = s_s.v
                ex = s_s.v
                self.wait("t", s_s, ex)
                self.emit("t", lambda E, kc=kc, asl=asl: E.matmul(
                    self.pb[2][0:65, :], self.big[:, 65 * kc:65 * (kc + 1)],
                    self.a32[:, 512 * asl:512 * (asl + 1)],
                    start=(kc == 0), stop=(kc == NKC - 1)), s_t)
                aguard[asl] = s_t.v
            umm = s_t.v
            self.wait("v", s_t, umm)
            self.emit("v", lambda E: E.reciprocal(
                self.zb[0:1, :], self.pb[2][64:65, :]), s_v)
            self.wait("t", s_v)
            self.emit("t", lambda E: E.matmul(
                self.pb[3][0:64, :], self.ones[0:1, 0:64], self.zb[0:1, :],
                start=True, stop=True), s_t)
            zv = s_t.v
            self.wait("s", s_t, zv)
            self.waitd("s", getattr(self, "un_tok", None))
            self.emit("s", lambda E: E.copy(self.zb[0:64, :], self.pb[3][0:64, :]))
            self.emit("s", lambda E: E.copy(self.ustg[0:64, :], self.pb[2][0:64, :]),
                      s_s)
            self.wait("v", s_s)
            self.emit("v", lambda E: E.tensor_tensor(
                self.ustg[0:64, :], self.ustg[0:64, :], self.zb[0:64, :],
                ALU.mult), s_v)
            unv = s_v.v
            self.wait("g", s_v, unv)
            self.un_tok = self.dma("g", lambda E, hl=hl, tq=tq: E.dma_start(
                bass.AP(self.ag2in, hl * DH * LK + 512 * tq, [[LK, 64], [1, 512]]),
                self.ustg[0:64, :]))
            self.wait("t", s_s, s_s.v)

    def emit_wo(self):
        s_v, s_t, s_s, s_g, s_y = self.s_v, self.s_t, self.s_s, self.s_g, self.s_y

        m0 = self.mskv[:, 0:1]
        m1 = self.mskv[:, 1:2]
        vguard = s_v.v
        for g in range(H):
            grp = g // 8
            slot = g % 8
            cc = g // 2
            row0 = 64 * (g % 2)
            dst = self.ar[row0:row0 + 64, 2048 * cc:2048 * (cc + 1)]
            c1b = self.r1[row0:row0 + 64, 0:2048]
            self.wait("g", s_v, vguard)
            t0 = self.dma("g", lambda E, d=dst, grp=grp, slot=slot: E.dma_start(
                d, self.ag2l[grp:grp + 1, slot:slot + 1, :, 0:LQ]
                .squeeze(0).squeeze(0)))
            t1 = self.dma("g", lambda E, c=c1b, grp=grp, slot=slot: E.dma_start(
                c, self.ag2l[grp:grp + 1, slot:slot + 1, :, LQ:2 * LQ]
                .squeeze(0).squeeze(0)))
            self.waitd("v", t0)
            self.waitd("v", t1)
            self.emit("v", lambda E, d=dst, row0=row0: E.tensor_scalar(
                d, d, m0[row0:row0 + 64, :], None, ALU.mult))
            self.emit("v", lambda E, d=dst, c=c1b, row0=row0: E.scalar_tensor_tensor(
                d, c, m1[row0:row0 + 64, :], d, ALU.mult, ALU.add), s_v)
            vguard = s_v.v
        self.wait("y", s_v)
        self.wait("t", s_v)
        for c8 in range(NCC):
            self.dma("y", lambda E, c8=c8: E.dma_start(
                self.big[:, 1024 * c8:1024 * (c8 + 1)],
                self.woT[128 * c8:128 * (c8 + 1), :]))
        self.qfence("t")
        bank_guard = {}
        for d8 in range(NCC):
            for t4 in range(NT4):
                self.wait("t", s_v, bank_guard.get(t4, 0))
                for c8 in range(NCC):
                    last = (c8 == NCC - 1)
                    self.emit("t", lambda E, d8=d8, t4=t4, c8=c8, last=last:
                              E.matmul(
                        self.pb[t4][:, :],
                        self.big[:, 1024 * c8 + 128 * d8:1024 * c8 + 128 * (d8 + 1)],
                        self.ar[:, 2048 * c8 + 512 * t4:2048 * c8 + 512 * (t4 + 1)],
                        start=(c8 == 0), stop=last), s_t if last else None)
                mv = s_t.v
                self.wait("s", s_t, mv)
                self.emit("s", lambda E, t4=t4, d8=d8: E.activation(
                    self.r1[:, 512 * t4:512 * (t4 + 1)], self.pb[t4][:, :],
                    AF.Identity, bias=self.bo_v[:, d8:d8 + 1], scale=1.0), s_s)
                ev = s_s.v
                for j4 in range(4):
                    bi = 6 + (j4 % 2)
                    self.wait("t", s_s, ev)
                    self.wait("t", s_v, bank_guard.get(bi, 0))
                    self.emit("t", lambda E, bi=bi, t4=t4, j4=j4: E.transpose(
                        self.pb[bi][:, 0:128],
                        self.r1[:, 512 * t4 + 128 * j4:512 * t4 + 128 * (j4 + 1)],
                        self.ident[:, :]), s_t)
                    self.wait("v", s_t)
                    ti = 4 * t4 + j4
                    dst = self.x[:, D * ti + 128 * d8:D * ti + 128 * (d8 + 1)]
                    self.emit("v", lambda E, bi=bi, dd=dst: E.tensor_tensor(
                        dd, dd, self.pb[bi][:, 0:128], ALU.add), s_v)
                    bank_guard[bi] = s_v.v
                bank_guard[t4] = s_v.v


_NC_CACHE = {}


def _build(taps=()):
    key = tuple((n, tuple(s)) for n, s in taps)
    if key not in _NC_CACHE:
        _NC_CACHE[key] = Build(taps).build()
    return _NC_CACHE[key]


def _bucket_table():
    d = np.arange(-4095, 4096)
    n = d
    nf = np.maximum(n, 1).astype(np.float32)
    val = (np.log(nf / np.float32(16.0)).astype(np.float32)
           / np.float32(np.log(8.0)) * np.float32(16.0)).astype(np.int32) + 16
    val = np.minimum(val, 31)
    ret = np.where(n < 16, n, val)
    return np.maximum(ret, 0).astype(np.int64)


def _make_btab(rel_bias):
    bk = _bucket_table()
    kkk = np.arange(128)[:, None]
    qqq = np.arange(512)[None, :]
    out = np.empty((H, NBAND, 128, 512), dtype=np.float32)
    for s in range(NBAND):
        D0 = 128 - 128 * s
        idx = bk[(D0 + qqq - kkk) + 4095]
        out[:, s] = np.transpose(rel_bias[idx], (2, 0, 1))
    return np.ascontiguousarray(out)


def kernel(L0_attn_x, L0_attn_wq, L0_attn_bq, L0_attn_wk, L0_attn_bk,
           L0_attn_wv, L0_attn_bv, L0_attn_wo, L0_attn_bo, L0_attn_rel_bias,
           L0_ffn_W_in, L0_ffn_b_in, L0_ffn_W_out, L0_ffn_b_out, _taps=()):
    nc = _build(_taps)
    x = np.ascontiguousarray(np.asarray(L0_attn_x, dtype=np.float32))
    com = {
        "wqT": np.asarray(L0_attn_wq).T,
        "wkT": np.asarray(L0_attn_wk).T,
        "wvT": np.asarray(L0_attn_wv).T,
        "woT": np.asarray(L0_attn_wo).T,
        "bqd": np.asarray(L0_attn_bq), "bkd": np.asarray(L0_attn_bk),
        "bvd": np.asarray(L0_attn_bv), "bod": np.asarray(L0_attn_bo),
        "WinT": np.asarray(L0_ffn_W_in).T,
        "bind": np.asarray(L0_ffn_b_in),
        "Wout": np.asarray(L0_ffn_W_out),
        "boutd": np.asarray(L0_ffn_b_out),
        "rbT": np.asarray(L0_attn_rel_bias).T,
        "btab": _make_btab(np.asarray(L0_attn_rel_bias)),
    }
    com = {k: np.ascontiguousarray(np.asarray(v, dtype=np.float32))
           for k, v in com.items()}
    in_maps = []
    for c in range(8):
        b, j = c // 2, c % 2
        m = dict(com)
        m["mid"] = np.array([[c]], dtype=np.int32)
        mk = np.zeros((128, 2), dtype=np.float32)
        mk[:, j] = 1.0
        m["msk"] = mk
        m["xin"] = np.ascontiguousarray(x[b, 2048 * j:2048 * (j + 1), :])
        in_maps.append(m)
    res = run_bass_kernel_spmd(nc, in_maps, core_ids=list(range(8)))
    full = np.empty((4, 4096, 1024), dtype=np.float32)
    for c in range(8):
        b, j = c // 2, c % 2
        full[b, 2048 * j:2048 * (j + 1), :] = res.results[c]["out"]
    if _taps:
        return full, res
    return full


# version 08 (final)
# speedup vs baseline: 1.2388x; 1.2388x over previous
import numpy as np
from contextlib import ExitStack

import concourse.bass as bass
import concourse.mybir as mybir
from concourse.bass_utils import run_bass_kernel_spmd

F32 = mybir.dt.float32
I32 = mybir.dt.int32
AF = mybir.ActivationFunctionType
ALU = mybir.AluOpType
AXX = mybir.AxisListType.X

C64 = 1.5 * 2.0 ** 52
C32 = 1.5 * 2.0 ** 23

D = 1024
DFF = 4096
LQ = 2048
LK = 4096
NTI = 16
NCC = 8
NFC = 32
NT4 = 4
NQC = 8
NKC = 32
HLOC = 8
H = 16
DH = 64
EPS = 1e-5
NBAND = 5
HSZ = DH * LQ

S_RB7 = 16
S_EPS = 17
S_SQ = 18
S_MEAN = 19
S_E2 = 20
S_VAR = 21
S_RSTD = 22
S_NEGMB = 23
S_AR = 24
S_REC = 25
S_SX = 26
S_SH = 27
S_AMY = 28
S_AMH = 29
S_WA = 30
S_SF = 31
S_GAM = 32
S_T1 = 33
S_T2 = 34
S_NCB = 35
S_NC0 = 36
S_NC31 = 37
S_RSTD2 = 38
S_GAM2 = 39


class Sem:
    def __init__(self, h):
        self.h = h
        self.v = 0

    def bump(self, n=1):
        self.v += n
        return self.v


class Build:
    def __init__(self, taps=()):
        nc = bass.Bass("TRN2", target_bir_lowering=False, detect_race_conditions=True)
        self.nc = nc
        self.cmd = {"t": [], "v": [], "s": [], "g": [], "y": []}
        self.regs = {}
        d = nc.dram_tensor
        self.mid = d("mid", [1, 1], I32, kind="ExternalInput")
        self.xin = d("xin", [LQ, D], F32, kind="ExternalInput")
        self.wqT = d("wqT", [D, D], F32, kind="ExternalInput")
        self.wkT = d("wkT", [D, D], F32, kind="ExternalInput")
        self.wvT = d("wvT", [D, D], F32, kind="ExternalInput")
        self.woT = d("woT", [D, D], F32, kind="ExternalInput")
        self.bqd = d("bqd", [D], F32, kind="ExternalInput")
        self.bkd = d("bkd", [D], F32, kind="ExternalInput")
        self.bvd = d("bvd", [D], F32, kind="ExternalInput")
        self.bod = d("bod", [D], F32, kind="ExternalInput")
        self.WinT = d("WinT", [DFF, D], F32, kind="ExternalInput")
        self.bind = d("bind", [DFF], F32, kind="ExternalInput")
        self.Wout = d("Wout", [DFF, D], F32, kind="ExternalInput")
        self.boutd = d("boutd", [D], F32, kind="ExternalInput")
        self.rbT = d("rbT", [H, 32], F32, kind="ExternalInput")
        self.btab = d("btab", [H, NBAND, 128, 512], F32, kind="ExternalInput")
        self.out = d("out", [LQ, D], F32, kind="ExternalOutput")
        self.hid = d("hid", [NFC * 128 * LQ], F32)
        self.nsp = d("nsp", [NFC * 128 * D], F32)
        self.agin = d("agin", [3 * H * HSZ], F32)
        self.agout = d("agout", [2, 3, H, HSZ], F32)
        self.ag2in = d("ag2in", [HLOC * DH * LK], F32)
        self.ag2out = d("ag2out", [2, HLOC, DH, LK], F32)
        self.arin = d("arin", [4, 1], F32)
        self.arout = d("arout", [4, 1], F32)
        self.msk = d("msk", [128, 2], F32, kind="ExternalInput")
        self.agl = self.agout
        self.ag2l = self.ag2out
        self.taps = dict(taps)
        self.tapd = {}
        for name, shape in self.taps.items():
            self.tapd[name] = d(name, shape, F32, kind="ExternalOutput")

    def emit(self, e, fn, sem=None, inc=1):
        if e == "v":
            own = self.s_v
        elif e == "s":
            own = self.s_s
        else:
            own = None
        if own is not None:
            assert sem is None or sem is own, "v/s ops must milestone their own sem"
            if own.v > 0:
                self.cmd[e].append(lambda E, s=own.h, vv=own.v: E.wait_ge(s, vv))
            own.bump(1)
            self.cmd[e].append(lambda E, fn=fn, s=own.h: fn(E).then_inc(s, 1))
            return
        if sem is not None:
            sem.bump(inc)
            self.cmd[e].append(lambda E, fn=fn, s=sem.h, i=inc: fn(E).then_inc(s, i))
        else:
            self.cmd[e].append(fn)

    def wait(self, e, sem, val=None):
        v = sem.v if val is None else val
        if v > 0:
            self.cmd[e].append(lambda E, s=sem.h, vv=v: E.wait_ge(s, vv))

    def barrier(self):
        b = self.s_bar
        for e in self.cmd:
            self.wait(e, self.s_g)
            self.wait(e, self.s_y)
            self.wait(e, self.s_cc)
            self.wait(e, self.s_t)
            self.wait(e, self.s_v)
            self.wait(e, self.s_s)
            self.wait(e, self.s_gp)
            self.wait(e, self.s_ar)
            for s in self.s_w + self.s_h2 + self.s_p + self.s_gq + self.s_yq:
                self.wait(e, s)
        for _ in range(5):
            b.v += 1
        tgt = b.v
        for e in self.cmd:
            self.cmd[e].append(lambda E, s=b.h: E.nop().then_inc(s, 1))
        for e in self.cmd:
            self.cmd[e].append(lambda E, s=b.h, vv=tgt: E.wait_ge(s, vv))

    def dma(self, q, fn):
        pool = self.s_gq if q == "g" else self.s_yq
        par = self.qpar[q]
        self.qpar[q] ^= 1
        s = pool[par]
        self.wait(q, s)
        self.emit(q, fn, s, 16)
        return (s, s.v)

    def waitd(self, e, tok):
        if tok is not None:
            self.wait(e, tok[0], tok[1])

    def qfence(self, e):
        for s in self.s_gq + self.s_yq:
            self.wait(e, s)

    def col(self, i):
        return self.sm[:, i:i + 1]

    def tap(self, name, ap_fn):
        if name not in self.tapd:
            return
        t = self.tapd[name]
        self.barrier()
        self.dma("g", lambda E, o=t, f=ap_fn: E.dma_start(o.ap(), f()))
        self.barrier()

    def build(self):
        nc = self.nc
        with ExitStack() as stk:
            sem = lambda n: Sem(stk.enter_context(nc.semaphore(n)))
            sb = lambda n, shape, dt=F32: stk.enter_context(nc.sbuf_tensor(n, shape, dt))
            self.s_y = sem("s_y")
            self.s_g = sem("s_g")
            self.s_cc = sem("s_cc")
            self.s_t = sem("s_t")
            self.s_v = sem("s_v")
            self.s_s = sem("s_s")
            self.s_gp = sem("s_gp")
            self.s_bar = sem("s_bar")
            self.s_ar = sem("s_ar")
            self.s_gq = [sem("s_gq0"), sem("s_gq1")]
            self.s_yq = [sem("s_yq0"), sem("s_yq1")]
            self.qpar = {"g": 0, "y": 0}
            self.s_w = [sem("s_w0"), sem("s_w1")]
            self.s_h2 = [sem("s_h0"), sem("s_h1")]
            self.s_p = [sem("s_p0"), sem("s_p1")]
            self.x = sb("x", [128, NTI * D])
            self.ar = sb("arn", [128, 16384])
            self.r1 = sb("r1", [128, 2048])
            self.r2 = sb("r2", [128, 2048])
            self.r3 = sb("r3", [128, 2048])
            self.big = sb("big", [128, 8192])
            self.ones = sb("ones1", [128, 128])
            self.ident = sb("ident", [128, 128])
            self.io_r = sb("io_r", [128, 128])
            self.binv = sb("binv", [128, 32])
            self.bovh = sb("bovh", [128, 8])
            self.bqv = sb("bqv", [128, 8])
            self.bkv = sb("bkv", [128, 8])
            self.bvv = sb("bvv", [128, 8])
            self.bo_v = sb("bo_v", [128, 8])
            self.srv = sb("srv", [128, 32])
            self.gmv = sb("gmv", [128, 32])
            self.nq2 = sb("nq2", [128, 16])
            self.nk2 = sb("nk2", [128, 16])
            self.nq8 = sb("nq8", [128, 8])
            self.nk8 = sb("nk8", [128, 8])
            self.mskv = sb("mskv", [128, 2])
            self.sm = sb("sm", [128, 64])
            self.rbh = sb("rbh", [1, 64])
            self.a32 = sb("a32", [128, 1024])
            self.zb = sb("zb", [128, 512])
            self.ustg = sb("ustg", [128, 512])
            self.idt = sb("idt", [1, 1], I32)
            self.pb = [stk.enter_context(nc.psum_tensor("pb%d" % i, [128, 512], F32))
                       for i in range(8)]

            self.emit_all()

            blk = stk.enter_context(nc.Block())

            @blk.tensor
            def _(E):
                for f in self.cmd["t"]:
                    f(E)

            @blk.vector
            def _(E):
                for f in self.cmd["v"]:
                    f(E)

            @blk.scalar
            def _(E):
                for f in self.cmd["s"]:
                    f(E)

            @blk.gpsimd
            def _(E):
                for f in self.cmd["g"]:
                    f(E)

            @blk.sync
            def _(E):
                for f in self.cmd["y"]:
                    f(E)
        return nc

    def emit_all(self):
        s_y, s_g, s_v, s_t, s_s = self.s_y, self.s_g, self.s_v, self.s_t, self.s_s
        for ti in range(NTI):
            self.dma("g", lambda E, ti=ti: E.dma_start(
                self.x[:, D * ti:D * (ti + 1)], self.xin[128 * ti:128 * (ti + 1), :]))
        self.dma("g", lambda E: E.dma_start(
            self.binv[:, :], bass.AP(self.bind, 0, [[1, 128], [128, 32]]),
            allow_slow_non_contiguous=True))
        for vec, dst in [(self.boutd, self.bovh), (self.bqd, self.bqv),
                         (self.bkd, self.bkv), (self.bvd, self.bvv),
                         (self.bod, self.bo_v)]:
            self.dma("g", lambda E, v=vec, dd=dst: E.dma_start(
                dd[:, :], bass.AP(v, 0, [[1, 128], [128, 8]]),
                allow_slow_non_contiguous=True))
        self.dma("g", lambda E: E.dma_start(self.ar[0:16, 0:32], self.rbT[:, :]))
        idtok = self.dma("g", lambda E: E.dma_start(self.idt[:, :], self.mid[:, :]))
        self.dma("g", lambda E: E.dma_start(self.mskv[:, :], self.msk[:, :]))
        self.emit("g", lambda E: E.memset(self.ones[:, :], 1.0))
        self.emit("g", lambda E: E.memset(self.col(S_EPS), EPS))
        self.emit("g", lambda E: E.iota(
            bass.AP(self.io_r, 0, [[128, 128], [1, 128]]), [[1, 128]],
            channel_multiplier=0, allow_small_or_imprecise_dtypes=True))
        self.emit("g", lambda E: E.iota(
            bass.AP(self.ident, 0, [[128, 128], [1, 128]]), [[0, 128]],
            channel_multiplier=1, allow_small_or_imprecise_dtypes=True), self.s_gp)

        self.qfence("g")
        _idsem, _idval = idtok

        def mkregs(E):
            E.wait_ge(_idsem.h, _idval)
            rk = E.alloc_register(name="rk")
            E.load(rk, self.idt[0:1, 0:1])
            jj = E.alloc_register(name="jj")
            E.reg_mod(jj, rk, 2)
            r0 = E.alloc_register(name="r0")
            E.reg_sub(r0, rk, jj)
            self.regs.update(rk=rk, jj=jj, r0=r0)
        self.cmd["g"].append(mkregs)

        self.qfence("v")
        self.wait("v", self.s_gp)
        self.emit("v", lambda E: E.tensor_tensor(
            self.ident[:, :], self.ident[:, :], self.io_r[:, :], ALU.is_equal))
        self.emit("v", lambda E: E.tensor_scalar(
            self.bovh[:, :], self.bovh[:, :], 0.5, None, ALU.mult))
        self.emit("v", lambda E: E.tensor_scalar(
            self.bkv[:, :], self.bkv[:, :], 0.125, None, ALU.mult))
        self.emit("v", lambda E: E.tensor_reduce(
            self.ar[0:16, 32:33], self.ar[0:16, 0:32], axis=AXX, op=ALU.max), s_v)
        self.wait("t", s_v)
        self.emit("t", lambda E: E.transpose(
            self.pb[7][0:1, 0:16], self.ar[0:16, 32:33], self.ident[0:16, 0:16]), s_t)
        self.wait("v", s_t)
        self.emit("v", lambda E: E.tensor_reduce(
            self.ar[0:1, 33:34], self.pb[7][0:1, 0:16], axis=AXX, op=ALU.max), s_v)
        self.wait("t", s_v)
        self.emit("t", lambda E: E.matmul(
            self.pb[7][:, 0:1], self.ones[0:1, 0:128], self.ar[0:1, 33:34],
            start=True, stop=True), s_t)
        self.wait("v", s_t)
        self.emit("v", lambda E: E.tensor_scalar(
            self.col(S_RB7), self.pb[7][:, 0:1], -1.0, 7.0, ALU.mult, ALU.add), s_v)
        self.barrier()

        self.tap("tap_x0", lambda: self.x[:, 0:16384])
        self.ffn(0)
        self.attn()
        self.ffn(1)

        self.barrier()
        for ti in range(NTI):
            self.dma("y", lambda E, ti=ti: E.dma_start(
                self.out[128 * ti:128 * (ti + 1), :], self.x[:, D * ti:D * (ti + 1)]))
        self.qfence("y")

    def layernorm_to_arena(self, amax=False):
        s_v, s_t, s_s = self.s_v, self.s_t, self.s_s
        if amax:
            self.emit("g", lambda E: E.memset(self.col(S_AMY), 0.0), self.s_gp)
            self.wait("v", self.s_gp)
        guard = {}
        tprev = [0]
        for ti in range(NTI):
            xt = self.x[:, D * ti:D * (ti + 1)]
            self.wait("v", s_t, tprev[0])
            self.emit("v", lambda E, xt=xt: E.tensor_reduce(
                self.col(S_MEAN), xt, axis=AXX, op=ALU.add))
            self.emit("v", lambda E: E.tensor_scalar(
                self.col(S_MEAN), self.col(S_MEAN), 1.0 / D, None, ALU.mult), s_v)
            self.wait("s", s_v)
            self.emit("s", lambda E, xt=xt: E.activation(
                self.r1[:, 0:D], xt, AF.Square), s_s)
            self.wait("v", s_s)
            self.emit("v", lambda E: E.tensor_reduce(
                self.col(S_E2), self.r1[:, 0:D], axis=AXX, op=ALU.add))
            self.emit("v", lambda E: E.tensor_scalar(
                self.col(S_E2), self.col(S_E2), 1.0 / D, None, ALU.mult))
            self.emit("v", lambda E: E.tensor_scalar(
                self.col(S_VAR), self.col(S_MEAN), self.col(S_MEAN), None, ALU.mult))
            self.emit("v", lambda E: E.tensor_tensor(
                self.col(S_VAR), self.col(S_E2), self.col(S_VAR), ALU.subtract), s_v)
            self.wait("s", s_v)
            self.emit("s", lambda E: E.activation(
                self.col(S_RSTD), self.col(S_VAR), AF.Sqrt, bias=self.col(S_EPS)), s_s)
            self.wait("v", s_s)
            self.emit("v", lambda E: E.reciprocal(self.col(S_RSTD2), self.col(S_RSTD)))
            self.emit("v", lambda E: E.tensor_scalar(
                self.col(S_NEGMB), self.col(S_MEAN), self.col(S_RSTD2), -1.0,
                ALU.mult, ALU.mult))
            self.emit("v", lambda E, xt=xt: E.tensor_scalar(
                self.r2[:, 0:D], xt, self.col(S_RSTD2), self.col(S_NEGMB),
                ALU.mult, ALU.add), s_v)
            if amax:
                self.emit("v", lambda E: E.tensor_reduce(
                    self.col(S_SQ), self.r2[:, 0:D], axis=AXX, op=ALU.max,
                    apply_absolute_value=True))
                self.emit("v", lambda E: E.tensor_tensor(
                    self.col(S_AMY), self.col(S_AMY), self.col(S_SQ), ALU.max), s_v)
            yv = s_v.v
            for cc in range(NCC):
                bi = 4 + (cc % 2)
                self.wait("t", s_v, max(yv, guard.get(bi, 0)))
                self.emit("t", lambda E, bi=bi, cc=cc: E.transpose(
                    self.pb[bi][:, 0:128], self.r2[:, 128 * cc:128 * (cc + 1)],
                    self.ident[:, :]), s_t)
                self.wait("v", s_t)
                dst = self.ar[:, 2048 * cc + 128 * ti: 2048 * cc + 128 * (ti + 1)]
                self.emit("v", lambda E, bi=bi, dd=dst: E.tensor_copy(
                    dd, self.pb[bi][:, 0:128]), s_v)
                guard[bi] = s_v.v
            tprev[0] = s_t.v

    def fold_allreduce(self, amax_col, slot, out_col):
        s_v, s_t, s_g = self.s_v, self.s_t, self.s_g
        self.barrier()
        self.emit("t", lambda E, c=amax_col: E.transpose(
            self.pb[7][0:1, 0:128], self.col(c), self.ident[:, :]), s_t)
        self.wait("v", s_t)
        self.emit("v", lambda E: E.tensor_reduce(
            self.sm[0:1, S_AR:S_AR + 1], self.pb[7][0:1, 0:128], axis=AXX,
            op=ALU.max))
        self.emit("v", lambda E: E.tensor_scalar(
            self.sm[0:1, S_AR:S_AR + 1], self.sm[0:1, S_AR:S_AR + 1], 1e-8, None,
            ALU.max), s_v)
        self.wait("g", s_v)
        self.emit("g", lambda E, sl=slot: E.dma_start(
            self.arin[sl:sl + 1, :], self.sm[0:1, S_AR:S_AR + 1]), self.s_ar, 16)
        self.wait("g", self.s_ar)
        self.emit("g", lambda E, sl=slot: E.collective_compute(
            "AllReduce", ALU.max, replica_groups=[list(range(8))],
            ins=[self.arin[sl:sl + 1, :]], outs=[self.arout[sl:sl + 1, :]]),
            self.s_cc)
        self.wait("g", self.s_cc)
        self.emit("g", lambda E, sl=slot: E.dma_start(
            self.sm[0:1, S_AR:S_AR + 1], self.arout[sl:sl + 1, :]), self.s_ar, 16)
        self.wait("t", self.s_ar)
        self.emit("t", lambda E: E.matmul(
            self.pb[7][:, 0:1], self.ones[0:1, 0:128], self.sm[0:1, S_AR:S_AR + 1],
            start=True, stop=True), s_t)
        self.wait("v", s_t)
        self.emit("v", lambda E: E.reciprocal(self.col(S_REC), self.pb[7][:, 0:1]))
        self.emit("v", lambda E, oc=out_col: E.tensor_scalar(
            self.col(oc), self.col(S_REC), 32767.0, None, ALU.mult), s_v)
        self.barrier()

    def round_chain(self, src, dst, scol, s0, width):
        u = self.r1[:, 0:width]
        q = self.r3[:, 0:width]
        self.emit("v", lambda E, s=src, u=u, sc=scol, s0=s0: E.tensor_scalar(
            u, s, sc, s0, ALU.mult, ALU.add))
        self.emit("v", lambda E, u=u, q=q: E.tensor_scalar(
            q, u, C64, -C64, ALU.add, ALU.add))
        self.emit("v", lambda E, u=u, q=q: E.tensor_tensor(u, u, q, ALU.subtract))
        self.emit("v", lambda E, u=u: E.tensor_scalar(
            u, u, C32, -C32, ALU.add, ALU.add))
        self.emit("v", lambda E, u=u: E.scalar_tensor_tensor(
            u, u, 1.0, u, ALU.is_ge, ALU.mult))
        self.emit("v", lambda E, q=q, u=u, dd=dst, s0=s0: E.scalar_tensor_tensor(
            dd, q, -s0, u, ALU.add, ALU.add), self.s_v)

    def ffn(self, idx):
        s_v, s_t, s_s, s_g, s_y = self.s_v, self.s_t, self.s_s, self.s_g, self.s_y
        self.layernorm_to_arena(amax=True)
        self.barrier()
        self.tap("tap_yT%d" % idx, lambda: self.ar[:, 0:16384])
        self.fold_allreduce(S_AMY, 2 * idx, S_SX)
        for cc in range(NCC):
            a = self.ar[:, 2048 * cc:2048 * (cc + 1)]
            self.round_chain(a, a, self.col(S_SX), 40000.0, 2048)
        self.barrier()
        self.tap("tap_M%d" % idx, lambda: self.ar[:, 0:16384])

        self.emit("g", lambda E: E.memset(self.col(S_AMH), 0.0), self.s_gp)
        self.wait("v", self.s_gp)
        wsl = [self.big[:, 0:1024], self.big[:, 1024:2048]]
        ntl = [self.big[:, 2048:3072], self.big[:, 3072:4096]]
        hsl = [self.big[:, 4096:6144], self.big[:, 6144:8192]]
        w_free = [0, 0]
        h_free = [0, 0]
        bank_free = [0, 0, 0, 0]
        tguard = {}
        for ff in range(NFC):
            pp = ff % 2
            self.wait("y", s_t, w_free[pp])
            self.emit("y", lambda E, ff=ff, pp=pp: E.dma_start(
                wsl[pp], self.WinT[128 * ff:128 * (ff + 1), :]), self.s_w[pp], 16)
            self.wait("v", self.s_w[pp])
            w = wsl[pp]
            self.emit("v", lambda E, w=w: E.tensor_reduce(
                self.col(S_WA), w, axis=AXX, op=ALU.max, apply_absolute_value=True))
            self.emit("v", lambda E: E.tensor_scalar(
                self.col(S_WA), self.col(S_WA), 1e-8, None, ALU.max))
            self.emit("v", lambda E: E.reciprocal(self.col(S_SF), self.col(S_WA)))
            self.emit("v", lambda E: E.tensor_scalar(
                self.col(S_SF), self.col(S_SF), 127.0, None, ALU.mult))
            self.emit("v", lambda E: E.tensor_scalar(
                self.col(S_GAM2), self.col(S_SF), self.col(S_SX), None, ALU.mult))
            self.emit("v", lambda E: E.reciprocal(self.col(S_GAM), self.col(S_GAM2)),
                      s_v)
            self.round_chain(w, w, self.col(S_SF), 384.0, 1024)
            nq = s_v.v
            for c8 in range(NCC):
                bi = 4 + (c8 % 2)
                self.wait("t", s_v, max(nq, tguard.get(bi, 0)))
                self.emit("t", lambda E, bi=bi, w=w, c8=c8: E.transpose(
                    self.pb[bi][:, 0:128], w[:, 128 * c8:128 * (c8 + 1)],
                    self.ident[:, :]), s_t)
                self.wait("v", s_t)
                self.emit("v", lambda E, bi=bi, c8=c8, pp=pp: E.tensor_copy(
                    ntl[pp][:, 128 * c8:128 * (c8 + 1)], self.pb[bi][:, 0:128]), s_v)
                tguard[bi] = s_v.v
            w_free[pp] = s_t.v
            ntv = s_v.v
            self.wait("t", s_v, ntv)
            for t4 in range(NT4):
                self.wait("t", s_s, bank_free[t4])
            for c8 in range(NCC):
                for t4 in range(NT4):
                    last = (c8 == NCC - 1 and t4 == NT4 - 1)
                    self.emit("t", lambda E, c8=c8, t4=t4, pp=pp: E.matmul(
                        self.pb[t4][:, :], ntl[pp][:, 128 * c8:128 * (c8 + 1)],
                        self.ar[:, 2048 * c8 + 512 * t4: 2048 * c8 + 512 * (t4 + 1)],
                        start=(c8 == 0), stop=(c8 == NCC - 1)),
                        s_t if last else None)
            mmv = s_t.v
            self.wait("s", s_t, mmv)
            self.wait("s", s_v, s_v.v)
            self.wait("s", self.s_p[pp], h_free[pp])
            for t4 in range(NT4):
                self.emit("s", lambda E, t4=t4, pp=pp, ff=ff: E.activation(
                    hsl[pp][:, 512 * t4:512 * (t4 + 1)], self.pb[t4][:, :], AF.Gelu,
                    bias=self.binv[:, ff:ff + 1], scale=self.col(S_GAM)), s_s)
                bank_free[t4] = s_s.v
            sv = s_s.v
            self.wait("v", s_s, sv)
            self.emit("v", lambda E, pp=pp: E.tensor_reduce(
                self.col(S_T1), hsl[pp], axis=AXX, op=ALU.max,
                apply_absolute_value=True))
            self.emit("v", lambda E: E.tensor_tensor(
                self.col(S_AMH), self.col(S_AMH), self.col(S_T1), ALU.max), s_v)
            self.wait("g", s_s, sv)
            self.emit("g", lambda E, ff=ff, pp=pp: E.dma_start(
                bass.AP(self.hid, ff * 128 * LQ, [[LQ, 128], [1, LQ]]), hsl[pp]),
                self.s_p[pp], 16)
            h_free[pp] = self.s_p[pp].v
        self.barrier()

        self.fold_allreduce(S_AMH, 2 * idx + 1, S_SH)
        wq_free = [None, None]
        for rr in range(NFC):
            pp = rr % 2
            self.waitd("y", wq_free[pp])
            self.emit("y", lambda E, rr=rr, pp=pp: E.dma_start(
                wsl[pp], self.Wout[128 * rr:128 * (rr + 1), :]), self.s_w[pp], 16)
            self.wait("v", self.s_w[pp])
            w = wsl[pp]
            self.emit("v", lambda E, w=w: E.tensor_reduce(
                self.col(S_WA), w, axis=AXX, op=ALU.max, apply_absolute_value=True))
            self.emit("v", lambda E: E.tensor_scalar(
                self.col(S_WA), self.col(S_WA), 1e-8, None, ALU.max))
            self.emit("v", lambda E: E.reciprocal(self.col(S_SF), self.col(S_WA)))
            self.emit("v", lambda E, rr=rr: E.tensor_scalar(
                self.srv[:, rr:rr + 1], self.col(S_SF), 127.0, None, ALU.mult), s_v)
            self.round_chain(w, w, self.srv[:, rr:rr + 1], 384.0, 1024)
            self.wait("g", s_v)
            wq_free[pp] = self.dma("g", lambda E, rr=rr, w=w: E.dma_start(
                bass.AP(self.nsp, rr * 128 * D, [[D, 128], [1, D]]), w))
        self.barrier()
        self.emit("v", lambda E: E.tensor_scalar(
            self.nq2[:, 0:16], self.srv[:, 0:16], self.col(S_SH), None, ALU.mult))
        self.emit("v", lambda E: E.tensor_scalar(
            self.nk2[:, 0:16], self.srv[:, 16:32], self.col(S_SH), None, ALU.mult))
        self.emit("v", lambda E: E.reciprocal(self.gmv[:, 0:16], self.nq2[:, 0:16]))
        self.emit("v", lambda E: E.reciprocal(self.gmv[:, 16:32], self.nk2[:, 0:16]),
                  s_v)
        self.barrier()

        nsl = [self.big[:, 0:1024], self.big[:, 1024:2048]]
        htl = [self.big[:, 2048 + 512 * i:2048 + 512 * (i + 1)] for i in range(2)]
        esl = self.big[:, 4096:8192]
        n_free = [0, 0]
        ht_free = [0, 0]
        m2g, m2gv = 0, 0
        for t4 in range(NT4):
            for rr in range(NFC):
                pp = rr % 2
                self.wait("y", s_t, n_free[pp])
                self.emit("y", lambda E, rr=rr, pp=pp: E.dma_start(
                    nsl[pp], bass.AP(self.nsp, rr * 128 * D, [[D, 128], [1, D]])),
                    self.s_w[pp], 16)
                nready = self.s_w[pp].v
                self.wait("g", s_t, ht_free[pp])
                self.emit("g", lambda E, rr=rr, t4=t4, pp=pp: E.dma_start(
                    htl[pp], bass.AP(self.hid, rr * 128 * LQ + 512 * t4,
                                     [[LQ, 128], [1, 512]])), self.s_h2[pp], 16)
                self.wait("v", self.s_h2[pp])
                self.round_chain(htl[pp], htl[pp], self.col(S_SH), 40000.0, 512)
                self.emit("v", lambda E, pp=pp, rr=rr: E.tensor_scalar(
                    htl[pp], htl[pp], self.gmv[:, rr:rr + 1], None, ALU.mult), s_v)
                self.wait("t", s_v)
                self.wait("t", self.s_w[pp], nready)
                if rr == 0:
                    self.wait("t", s_s, m2g)
                    self.wait("t", s_v, m2gv)
                for o8 in range(8):
                    last = (o8 == 7)
                    self.emit("t", lambda E, o8=o8, pp=pp, rr=rr: E.matmul(
                        self.pb[o8][:, :], nsl[pp][:, 128 * o8:128 * (o8 + 1)],
                        htl[pp], start=(rr == 0), stop=(rr == NFC - 1)),
                        s_t if last else None)
                n_free[pp] = s_t.v
                ht_free[pp] = s_t.v
            mm2 = s_t.v
            self.wait("s", s_t, mm2)
            for o8 in range(8):
                self.emit("s", lambda E, o8=o8: E.activation(
                    esl[:, 512 * o8:512 * (o8 + 1)], self.pb[o8][:, :], AF.Identity,
                    bias=self.bovh[:, o8:o8 + 1], scale=0.5), s_s)
            m2g = s_s.v
            ev = s_s.v
            guard = {}
            for o8 in range(8):
                for j4 in range(4):
                    bi = o8 % 2
                    self.wait("t", s_s, ev)
                    self.wait("t", s_v, guard.get(bi, 0))
                    self.emit("t", lambda E, bi=bi, o8=o8, j4=j4: E.transpose(
                        self.pb[bi][:, 0:128],
                        esl[:, 512 * o8 + 128 * j4: 512 * o8 + 128 * (j4 + 1)],
                        self.ident[:, :]), s_t)
                    self.wait("v", s_t)
                    ti = 4 * t4 + j4
                    dst = self.x[:, D * ti + 128 * o8: D * ti + 128 * (o8 + 1)]
                    self.emit("v", lambda E, bi=bi, dd=dst: E.tensor_tensor(
                        dd, dd, self.pb[bi][:, 0:128], ALU.add), s_v)
                    guard[bi] = s_v.v
            m2gv = s_v.v
        self.barrier()
        self.tap("tap_xf%d" % idx, lambda: self.x[:, 0:16384])

    def attn(self):
        s_v, s_t, s_s, s_g, s_y = self.s_v, self.s_t, self.s_s, self.s_g, self.s_y
        self.layernorm_to_arena(amax=False)
        self.barrier()
        self.tap("tap_yTa", lambda: self.ar[:, 0:16384])
        self.emit("g", lambda E: E.memset(self.nq2[:, :], 0.0))
        self.emit("g", lambda E: E.memset(self.nk2[:, :], 0.0), self.s_gp)
        self.wait("v", self.s_gp)
        self.wait("t", self.s_gp)
        wfull = self.big
        for wt, bias, kind in [(self.wqT, self.bqv, "q"), (self.wkT, self.bkv, "k"),
                               (self.wvT, self.bvv, "v")]:
            self.barrier()
            for c8 in range(NCC):
                self.dma("y", lambda E, wt=wt, c8=c8: E.dma_start(
                    wfull[:, 1024 * c8:1024 * (c8 + 1)],
                    wt[128 * c8:128 * (c8 + 1), :]))
            self.qfence("t")
            bank_guard = {}
            spl_guard = {}
            for d8 in range(NCC):
                for t4 in range(NT4):
                    self.wait("t", s_v, bank_guard.get(t4, 0))
                    for c8 in range(NCC):
                        last = (c8 == NCC - 1)
                        self.emit("t", lambda E, d8=d8, t4=t4, c8=c8, last=last:
                                  E.matmul(
                            self.pb[t4][:, :],
                            wfull[:, 1024 * c8 + 128 * d8: 1024 * c8 + 128 * (d8 + 1)],
                            self.ar[:, 2048 * c8 + 512 * t4: 2048 * c8 + 512 * (t4 + 1)],
                            start=(c8 == 0), stop=last), s_t if last else None)
                    mv = s_t.v
                    self.wait("v", s_t, mv)
                    for tk in spl_guard.get(t4, []):
                        self.waitd("v", tk)
                    self.wait("v", s_s, s_s.v)
                    sc = 0.125 if kind == "k" else 1.0
                    self.emit("v", lambda E, t4=t4, d8=d8, b=bias, sc=sc: E.tensor_scalar(
                        self.r1[:, 512 * t4:512 * (t4 + 1)], self.pb[t4][:, :],
                        sc, b[:, d8:d8 + 1], ALU.mult, ALU.add), s_v)
                    bank_guard[t4] = s_v.v
                    ev = s_v.v
                    if kind in ("q", "k"):
                        self.wait("s", s_v, ev)
                        self.emit("s", lambda E, t4=t4: E.activation(
                            self.r2[:, 512 * t4:512 * (t4 + 1)],
                            self.r1[:, 512 * t4:512 * (t4 + 1)], AF.Square), s_s)
                        self.wait("t", s_s)
                        for hh in range(2):
                            self.wait("t", s_v, bank_guard.get(5, 0))
                            self.emit("t", lambda E, hh=hh, t4=t4: E.matmul(
                                self.pb[5][:, :],
                                self.ones[64 * hh:64 * (hh + 1), 0:128],
                                self.r2[64 * hh:64 * (hh + 1),
                                        512 * t4:512 * (t4 + 1)],
                                start=True, stop=True), s_t)
                            self.wait("v", s_t)
                            dst = self.nq2 if kind == "q" else self.nk2
                            hcol = 2 * d8 + hh
                            self.emit("v", lambda E: E.tensor_reduce(
                                self.col(S_T2), self.pb[5][:, :], axis=AXX,
                                op=ALU.max))
                            self.emit("v", lambda E, dd=dst, hcol=hcol: E.tensor_tensor(
                                dd[:, hcol:hcol + 1], dd[:, hcol:hcol + 1],
                                self.col(S_T2), ALU.max), s_v)
                            bank_guard[5] = s_v.v
                        self.wait("g", s_v, ev)
                        kk = 0 if kind == "k" else 1
                        toks = []
                        for hh in range(2):
                            h = 2 * d8 + hh
                            off = kk * H * HSZ + h * HSZ + 512 * t4
                            toks.append(self.dma(
                                "g", lambda E, off=off, hh=hh, t4=t4: E.dma_start(
                                    bass.AP(self.agin, off, [[LQ, 64], [1, 512]]),
                                    self.r1[64 * hh:64 * (hh + 1),
                                            512 * t4:512 * (t4 + 1)])))
                        spl_guard[t4] = toks
                    else:
                        for j4 in range(4):
                            tc = 4 * t4 + j4
                            slot = tc % 16
                            self.wait("t", s_v, max(ev, bank_guard.get(6, 0)))
                            self.emit("t", lambda E, t4=t4, j4=j4: E.transpose(
                                self.pb[6][:, 0:128],
                                self.r1[:, 512 * t4 + 128 * j4:
                                        512 * t4 + 128 * (j4 + 1)],
                                self.ident[:, :]), s_t)
                            self.wait("v", s_t)
                            for tk in spl_guard.get(300 + slot, []):
                                self.waitd("v", tk)
                            self.emit("v", lambda E, sl=slot: E.tensor_copy(
                                self.r3[:, 128 * sl:128 * (sl + 1)],
                                self.pb[6][:, 0:128]), s_v)
                            bank_guard[6] = s_v.v
                            self.wait("g", s_v)
                            toks = []
                            for hh in range(2):
                                h = 2 * d8 + hh
                                off = 2 * H * HSZ + h * HSZ + 64 * tc
                                toks.append(self.dma(
                                    "g", lambda E, off=off, sl=slot, hh=hh:
                                    E.dma_start(
                                        bass.AP(self.agin, off,
                                                [[1024, 128], [1, 64]]),
                                        self.r3[:, 128 * sl + 64 * hh:
                                                128 * sl + 64 * (hh + 1)])))
                            spl_guard[300 + slot] = toks
        self.barrier()
        self.emit("g", lambda E: E.collective_compute(
            "AllGather", ALU.bypass,
            replica_groups=[[2 * b, 2 * b + 1] for b in range(4)],
            ins=[self.agin.ap().opt()], outs=[self.agout.ap().opt()]), self.s_cc)
        self.wait("g", self.s_cc)
        self.emit("v", lambda E: E.tensor_scalar(
            self.nq8[:, :], self.nq2[:, 0:8], self.mskv[:, 0:1], None, ALU.mult))
        self.emit("v", lambda E: E.scalar_tensor_tensor(
            self.nq8[:, :], self.nq2[:, 8:16], self.mskv[:, 1:2], self.nq8[:, :],
            ALU.mult, ALU.add))
        self.emit("v", lambda E: E.tensor_scalar(
            self.nk8[:, :], self.nk2[:, 0:8], self.mskv[:, 0:1], None, ALU.mult))
        self.emit("v", lambda E: E.scalar_tensor_tensor(
            self.nk8[:, :], self.nk2[:, 8:16], self.mskv[:, 1:2], self.nk8[:, :],
            ALU.mult, ALU.add), s_v)
        self.barrier()

        self._expg = {}
        for hl in range(HLOC):
            self.attn_head(hl)

        self.barrier()
        self.emit("g", lambda E: E.collective_compute(
            "AllGather", ALU.bypass,
            replica_groups=[[2 * b, 2 * b + 1] for b in range(4)],
            ins=[self.ag2in.ap().opt()], outs=[self.ag2out.ap().opt()]), self.s_cc)
        self.wait("g", self.s_cc)
        self.tap("tap_ag2l", lambda: self.ag2l.ap())
        self.barrier()
        self.emit_wo()
        self.barrier()
        self.tap("tap_xa", lambda: self.x[:, 0:16384])

    def attn_head(self, hl):
        s_v, s_t, s_s, s_g, s_y = self.s_v, self.s_t, self.s_s, self.s_g, self.s_y
        self.barrier()
        kap = self.ar[0:64, 0:4096]
        qap = self.ar[0:64, 4096:8192]

        m0 = self.mskv[:, 0:1]
        m1 = self.mskv[:, 1:2]
        jobs = []
        ag, bt = self.agl, self.btab
        for half in range(2):
            jobs.append((self.ar[0:64, 2048 * half:2048 * (half + 1)],
                         self.r1[0:64, 0:2048],
                         lambda E, h=half, hl=hl: ag[h:h + 1, 0:1, hl:hl + 1, :]
                         .squeeze(0).squeeze(0).squeeze(0)
                         .rearrange("(a b) -> a b", a=64),
                         lambda E, h=half, hl=hl: ag[h:h + 1, 0:1, 8 + hl:9 + hl, :]
                         .squeeze(0).squeeze(0).squeeze(0)
                         .rearrange("(a b) -> a b", a=64), 64))
        for half in range(2):
            jobs.append((self.ar[0:64, 4096 + 2048 * half:4096 + 2048 * (half + 1)],
                         self.r1[0:64, 0:2048],
                         lambda E, h=half, hl=hl: ag[h:h + 1, 1:2, hl:hl + 1, :]
                         .squeeze(0).squeeze(0).squeeze(0)
                         .rearrange("(a b) -> a b", a=64),
                         lambda E, h=half, hl=hl: ag[h:h + 1, 1:2, 8 + hl:9 + hl, :]
                         .squeeze(0).squeeze(0).squeeze(0)
                         .rearrange("(a b) -> a b", a=64), 64))
        for half in range(2):
            jobs.append((self.ar[:, 8192 + 1024 * half:8192 + 1024 * (half + 1)],
                         self.r1[:, 0:1024],
                         lambda E, h=half, hl=hl: ag[h:h + 1, 2:3, hl:hl + 1, :]
                         .squeeze(0).squeeze(0).squeeze(0)
                         .rearrange("(a b) -> a b", a=128),
                         lambda E, h=half, hl=hl: ag[h:h + 1, 2:3, 8 + hl:9 + hl, :]
                         .squeeze(0).squeeze(0).squeeze(0)
                         .rearrange("(a b) -> a b", a=128), 128))
        jobs.append((self.ar[:, 10240:12288], self.r2[:, 0:2048],
                     lambda E, hl=hl: bt[hl:hl + 1, :, :, :].squeeze(0)
                     .rearrange("s p q -> p (s q)")[:, 0:2048] if False else None,
                     None, 128))
        ldg = [0]

        def band_src(gidx, s0, s1):
            a = bt[gidx:gidx + 1, s0:s1, :, :].squeeze(0)
            return a.transpose([1, 0, 2])
        jobs = jobs[:-1]

        vguard = s_v.v
        for dst, c1buf, f0, f1, rows in jobs:
            self.wait("g", s_v, vguard)
            t0 = self.dma("g", lambda E, d=dst, f=f0: E.dma_start(d, f(E)))
            t1 = self.dma("g", lambda E, c=c1buf, f=f1: E.dma_start(c, f(E)))
            self.waitd("v", t0)
            self.waitd("v", t1)
            mm0 = m0[0:rows, :]
            mm1 = m1[0:rows, :]
            self.emit("v", lambda E, d=dst, mm0=mm0: E.tensor_scalar(
                d, d, mm0, None, ALU.mult))
            self.emit("v", lambda E, d=dst, c=c1buf, mm1=mm1: E.scalar_tensor_tensor(
                d, c, mm1, d, ALU.mult, ALU.add), s_v)
            vguard = s_v.v
        for s0, s1, c1buf in ((0, 4, self.r2[:, 0:2048]),
                              (4, 5, self.big[:, 2560:3072])):
            ns = s1 - s0
            dst3 = bass.AP(self.ar, 10240 + 512 * s0,
                           [[16384, 128], [512, ns], [1, 512]])
            c1b3 = (bass.AP(self.r2, 0, [[2048, 128], [512, ns], [1, 512]])
                    if s0 == 0 else
                    bass.AP(self.big, 2560, [[8192, 128], [512, ns], [1, 512]]))
            self.wait("g", s_v, vguard)
            t0 = self.dma("g", lambda E, d=dst3, s0=s0, s1=s1, hl=hl: E.dma_start(
                d, band_src(hl, s0, s1)))
            t1 = self.dma("g", lambda E, c=c1b3, s0=s0, s1=s1, hl=hl: E.dma_start(
                c, band_src(8 + hl, s0, s1)))
            self.waitd("v", t0)
            self.waitd("v", t1)
            dst = self.ar[:, 10240 + 512 * s0:10240 + 512 * s1]
            c1buf2 = c1buf[:, 0:512 * ns] if s0 == 0 else c1buf
            self.emit("v", lambda E, d=dst: E.tensor_scalar(
                d, d, m0, None, ALU.mult))
            self.emit("v", lambda E, d=dst, c=c1buf2: E.scalar_tensor_tensor(
                d, c, m1, d, ALU.mult, ALU.add), s_v)
            vguard = s_v.v
        t0 = self.dma("g", lambda E, hl=hl: E.dma_start(
            self.rbh[0:1, 0:32], self.rbT[hl:hl + 1, :]))
        t1 = self.dma("g", lambda E, hl=hl: E.dma_start(
            self.rbh[0:1, 32:64], self.rbT[8 + hl:9 + hl, :]))
        self.waitd("v", t0)
        self.waitd("v", t1)
        self.emit("v", lambda E: E.tensor_scalar(
            self.rbh[0:1, 0:32], self.rbh[0:1, 0:32], m0[0:1, :], None, ALU.mult))
        self.emit("v", lambda E: E.scalar_tensor_tensor(
            self.rbh[0:1, 0:32], self.rbh[0:1, 32:64], m1[0:1, :],
            self.rbh[0:1, 0:32], ALU.mult, ALU.add), s_v)
        self.wait("g", s_v)
        self.emit("g", lambda E: E.memset(
            bass.AP(self.big, 64, [[8192, 128], [65, 32], [1, 1]]), 1.0), self.s_gp)
        self.emit("v", lambda E: E.tensor_copy(
            bass.AP(self.big, 0, [[8192, 128], [65, 32], [1, 64]]),
            bass.AP(self.ar, 8192, [[16384, 128], [64, 32], [1, 64]])), s_v)
        self.emit("v", lambda E, hl=hl: E.tensor_scalar(
            self.col(S_T1), self.nq8[:, hl:hl + 1], self.nk8[:, hl:hl + 1], None,
            ALU.mult), s_v)
        self.wait("s", s_v)
        self.emit("s", lambda E: E.activation(
            self.col(S_T1), self.col(S_T1), AF.Sqrt), s_s)
        self.wait("v", s_s)
        self.emit("v", lambda E: E.scalar_tensor_tensor(
            self.col(S_NCB), self.col(S_T1), -1.0, self.col(S_RB7), ALU.mult,
            ALU.add), s_v)
        self.wait("t", s_v)
        self.emit("t", lambda E: E.matmul(
            self.pb[7][:, 0:2], self.ones[0:1, 0:128],
            bass.AP(self.rbh, 0, [[64, 1], [31, 2]]), start=True, stop=True), s_t)
        self.wait("v", s_t)
        self.emit("v", lambda E: E.tensor_tensor(
            self.col(S_NC0), self.col(S_NCB), self.pb[7][:, 0:1], ALU.add))
        self.emit("v", lambda E: E.tensor_tensor(
            self.col(S_NC31), self.col(S_NCB), self.pb[7][:, 1:2], ALU.add), s_v)
        prep = s_v.v
        self.wait("t", s_v, prep)
        self.wait("t", self.s_gp)
        self.wait("s", s_v, prep)

        aguard = [0, 0]
        pend = None
        for tq in range(NQC):
            for kc in range(NKC):
                sb_ = kc % 2
                asl = kc % 2
                self.wait("t", s_s, self._expg.get(sb_, 0))
                self.emit("t", lambda E, kc=kc, tq=tq, sb_=sb_: E.matmul(
                    self.pb[sb_][:, :], kap[:, 128 * kc:128 * (kc + 1)],
                    qap[:, 512 * tq:512 * (tq + 1)], start=True, stop=True), s_t)
                smm = s_t.v
                if pend is not None:
                    pex, pkc, pasl = pend
                    pend = None
                    self.wait("t", s_s, pex)
                    self.emit("t", lambda E, kc=pkc, asl=pasl: E.matmul(
                        self.pb[2][0:65, :], self.big[:, 65 * kc:65 * (kc + 1)],
                        self.a32[:, 512 * asl:512 * (asl + 1)],
                        start=(kc == 0), stop=False), s_t)
                    aguard[pasl] = s_t.v
                lo, hi = 4 * tq - 1, 4 * tq + 3
                if lo <= kc <= hi:
                    slot = kc - lo
                    if tq == 0:
                        slot = kc + 1
                    self.wait("v", s_t, smm)
                    self.emit("v", lambda E, sb_=sb_, sl=slot: E.tensor_tensor(
                        self.pb[sb_][:, :], self.pb[sb_][:, :],
                        self.ar[:, 10240 + 512 * sl:10240 + 512 * (sl + 1)],
                        ALU.add), s_v)
                    bias_col = self.col(S_NCB)
                    self.wait("s", s_v)
                elif kc < lo:
                    bias_col = self.col(S_NC31)
                    self.wait("s", s_t, smm)
                else:
                    bias_col = self.col(S_NC0)
                    self.wait("s", s_t, smm)
                self.wait("s", s_t, aguard[asl])
                self.emit("s", lambda E, sb_=sb_, asl=asl, bc=bias_col: E.activation(
                    self.a32[:, 512 * asl:512 * (asl + 1)], self.pb[sb_][:, :],
                    AF.Exp, bias=bc), s_s)
                self._expg[sb_] = s_s.v
                pend = (s_s.v, kc, asl)
            pex, pkc, pasl = pend
            pend = None
            self.wait("t", s_s, pex)
            self.emit("t", lambda E, kc=pkc, asl=pasl: E.matmul(
                self.pb[2][0:65, :], self.big[:, 65 * kc:65 * (kc + 1)],
                self.a32[:, 512 * asl:512 * (asl + 1)],
                start=False, stop=True), s_t)
            aguard[pasl] = s_t.v
            umm = s_t.v
            self.wait("v", s_t, umm)
            self.emit("v", lambda E: E.reciprocal(
                self.zb[0:1, :], self.pb[2][64:65, :]), s_v)
            self.wait("t", s_v)
            self.emit("t", lambda E: E.matmul(
                self.pb[3][0:64, :], self.ones[0:1, 0:64], self.zb[0:1, :],
                start=True, stop=True), s_t)
            zv = s_t.v
            self.wait("s", s_t, zv)
            self.waitd("s", getattr(self, "un_tok", None))
            self.emit("s", lambda E: E.copy(self.zb[0:64, :], self.pb[3][0:64, :]))
            self.emit("s", lambda E: E.copy(self.ustg[0:64, :], self.pb[2][0:64, :]),
                      s_s)
            self.wait("v", s_s)
            self.emit("v", lambda E: E.tensor_tensor(
                self.ustg[0:64, :], self.ustg[0:64, :], self.zb[0:64, :],
                ALU.mult), s_v)
            unv = s_v.v
            self.wait("g", s_v, unv)
            self.un_tok = self.dma("g", lambda E, hl=hl, tq=tq: E.dma_start(
                bass.AP(self.ag2in, hl * DH * LK + 512 * tq, [[LK, 64], [1, 512]]),
                self.ustg[0:64, :]))
            self.wait("t", s_s, s_s.v)

    def emit_wo(self):
        s_v, s_t, s_s, s_g, s_y = self.s_v, self.s_t, self.s_s, self.s_g, self.s_y

        m0 = self.mskv[:, 0:1]
        m1 = self.mskv[:, 1:2]
        vguard = s_v.v
        for g in range(H):
            grp = g // 8
            slot = g % 8
            cc = g // 2
            row0 = 64 * (g % 2)
            dst = self.ar[row0:row0 + 64, 2048 * cc:2048 * (cc + 1)]
            c1b = self.r1[row0:row0 + 64, 0:2048]
            self.wait("g", s_v, vguard)
            t0 = self.dma("g", lambda E, d=dst, grp=grp, slot=slot: E.dma_start(
                d, self.ag2l[grp:grp + 1, slot:slot + 1, :, 0:LQ]
                .squeeze(0).squeeze(0)))
            t1 = self.dma("g", lambda E, c=c1b, grp=grp, slot=slot: E.dma_start(
                c, self.ag2l[grp:grp + 1, slot:slot + 1, :, LQ:2 * LQ]
                .squeeze(0).squeeze(0)))
            self.waitd("v", t0)
            self.waitd("v", t1)
            self.emit("v", lambda E, d=dst, row0=row0: E.tensor_scalar(
                d, d, m0[row0:row0 + 64, :], None, ALU.mult))
            self.emit("v", lambda E, d=dst, c=c1b, row0=row0: E.scalar_tensor_tensor(
                d, c, m1[row0:row0 + 64, :], d, ALU.mult, ALU.add), s_v)
            vguard = s_v.v
        self.wait("y", s_v)
        self.wait("t", s_v)
        for c8 in range(NCC):
            self.dma("y", lambda E, c8=c8: E.dma_start(
                self.big[:, 1024 * c8:1024 * (c8 + 1)],
                self.woT[128 * c8:128 * (c8 + 1), :]))
        self.qfence("t")
        bank_guard = {}
        for d8 in range(NCC):
            for t4 in range(NT4):
                self.wait("t", s_v, bank_guard.get(t4, 0))
                for c8 in range(NCC):
                    last = (c8 == NCC - 1)
                    self.emit("t", lambda E, d8=d8, t4=t4, c8=c8, last=last:
                              E.matmul(
                        self.pb[t4][:, :],
                        self.big[:, 1024 * c8 + 128 * d8:1024 * c8 + 128 * (d8 + 1)],
                        self.ar[:, 2048 * c8 + 512 * t4:2048 * c8 + 512 * (t4 + 1)],
                        start=(c8 == 0), stop=last), s_t if last else None)
                mv = s_t.v
                self.wait("s", s_t, mv)
                self.emit("s", lambda E, t4=t4, d8=d8: E.activation(
                    self.r1[:, 512 * t4:512 * (t4 + 1)], self.pb[t4][:, :],
                    AF.Identity, bias=self.bo_v[:, d8:d8 + 1], scale=1.0), s_s)
                ev = s_s.v
                for j4 in range(4):
                    bi = 6 + (j4 % 2)
                    self.wait("t", s_s, ev)
                    self.wait("t", s_v, bank_guard.get(bi, 0))
                    self.emit("t", lambda E, bi=bi, t4=t4, j4=j4: E.transpose(
                        self.pb[bi][:, 0:128],
                        self.r1[:, 512 * t4 + 128 * j4:512 * t4 + 128 * (j4 + 1)],
                        self.ident[:, :]), s_t)
                    self.wait("v", s_t)
                    ti = 4 * t4 + j4
                    dst = self.x[:, D * ti + 128 * d8:D * ti + 128 * (d8 + 1)]
                    self.emit("v", lambda E, bi=bi, dd=dst: E.tensor_tensor(
                        dd, dd, self.pb[bi][:, 0:128], ALU.add), s_v)
                    bank_guard[bi] = s_v.v
                bank_guard[t4] = s_v.v


_NC_CACHE = {}


def _build(taps=()):
    key = tuple((n, tuple(s)) for n, s in taps)
    if key not in _NC_CACHE:
        _NC_CACHE[key] = Build(taps).build()
    return _NC_CACHE[key]


def _bucket_table():
    d = np.arange(-4095, 4096)
    n = d
    nf = np.maximum(n, 1).astype(np.float32)
    val = (np.log(nf / np.float32(16.0)).astype(np.float32)
           / np.float32(np.log(8.0)) * np.float32(16.0)).astype(np.int32) + 16
    val = np.minimum(val, 31)
    ret = np.where(n < 16, n, val)
    return np.maximum(ret, 0).astype(np.int64)


def _make_btab(rel_bias):
    bk = _bucket_table()
    kkk = np.arange(128)[:, None]
    qqq = np.arange(512)[None, :]
    out = np.empty((H, NBAND, 128, 512), dtype=np.float32)
    for s in range(NBAND):
        D0 = 128 - 128 * s
        idx = bk[(D0 + qqq - kkk) + 4095]
        out[:, s] = np.transpose(rel_bias[idx], (2, 0, 1))
    return np.ascontiguousarray(out)


def kernel(L0_attn_x, L0_attn_wq, L0_attn_bq, L0_attn_wk, L0_attn_bk,
           L0_attn_wv, L0_attn_bv, L0_attn_wo, L0_attn_bo, L0_attn_rel_bias,
           L0_ffn_W_in, L0_ffn_b_in, L0_ffn_W_out, L0_ffn_b_out, _taps=()):
    nc = _build(_taps)
    x = np.ascontiguousarray(np.asarray(L0_attn_x, dtype=np.float32))
    com = {
        "wqT": np.asarray(L0_attn_wq).T,
        "wkT": np.asarray(L0_attn_wk).T,
        "wvT": np.asarray(L0_attn_wv).T,
        "woT": np.asarray(L0_attn_wo).T,
        "bqd": np.asarray(L0_attn_bq), "bkd": np.asarray(L0_attn_bk),
        "bvd": np.asarray(L0_attn_bv), "bod": np.asarray(L0_attn_bo),
        "WinT": np.asarray(L0_ffn_W_in).T,
        "bind": np.asarray(L0_ffn_b_in),
        "Wout": np.asarray(L0_ffn_W_out),
        "boutd": np.asarray(L0_ffn_b_out),
        "rbT": np.asarray(L0_attn_rel_bias).T,
        "btab": _make_btab(np.asarray(L0_attn_rel_bias)),
    }
    com = {k: np.ascontiguousarray(np.asarray(v, dtype=np.float32))
           for k, v in com.items()}
    in_maps = []
    for c in range(8):
        b, j = c // 2, c % 2
        m = dict(com)
        m["mid"] = np.array([[c]], dtype=np.int32)
        mk = np.zeros((128, 2), dtype=np.float32)
        mk[:, j] = 1.0
        m["msk"] = mk
        m["xin"] = np.ascontiguousarray(x[b, 2048 * j:2048 * (j + 1), :])
        in_maps.append(m)
    res = run_bass_kernel_spmd(nc, in_maps, core_ids=list(range(8)))
    full = np.empty((4, 4096, 1024), dtype=np.float32)
    for c in range(8):
        b, j = c // 2, c % 2
        full[b, 2048 * j:2048 * (j + 1), :] = res.results[c]["out"]
    if _taps:
        return full, res
    return full


# version 09
# speedup vs baseline: 1.2758x; 1.0299x over previous
import numpy as np
from contextlib import ExitStack

import concourse.bass as bass
import concourse.mybir as mybir
from concourse.bass_utils import run_bass_kernel_spmd

F32 = mybir.dt.float32
I32 = mybir.dt.int32
AF = mybir.ActivationFunctionType
ALU = mybir.AluOpType
AXX = mybir.AxisListType.X

C64 = 1.5 * 2.0 ** 52
C32 = 1.5 * 2.0 ** 23

D = 1024
DFF = 4096
LQ = 2048
LK = 4096
NTI = 16
NCC = 8
NFC = 32
NT4 = 4
NQC = 8
NKC = 32
HLOC = 8
H = 16
DH = 64
EPS = 1e-5
NBAND = 5
HSZ = DH * LQ

S_RB7 = 16
S_EPS = 17
S_SQ = 18
S_MEAN = 19
S_E2 = 20
S_VAR = 21
S_RSTD = 22
S_NEGMB = 23
S_AR = 24
S_REC = 25
S_SX = 26
S_SH = 27
S_AMY = 28
S_AMH = 29
S_WA = 30
S_SF = 31
S_GAM = 32
S_T1 = 33
S_T2 = 34
S_NCB = 35
S_NC0 = 36
S_NC31 = 37
S_RSTD2 = 38
S_GAM2 = 39
S_GAMB = 40


class Sem:
    def __init__(self, h):
        self.h = h
        self.v = 0

    def bump(self, n=1):
        self.v += n
        return self.v


class Build:
    def __init__(self, taps=()):
        nc = bass.Bass("TRN2", target_bir_lowering=False, detect_race_conditions=True)
        self.nc = nc
        self.cmd = {"t": [], "v": [], "s": [], "g": [], "y": []}
        self.regs = {}
        d = nc.dram_tensor
        self.mid = d("mid", [1, 1], I32, kind="ExternalInput")
        self.xin = d("xin", [LQ, D], F32, kind="ExternalInput")
        self.wqT = d("wqT", [D, D], F32, kind="ExternalInput")
        self.wkT = d("wkT", [D, D], F32, kind="ExternalInput")
        self.wvT = d("wvT", [D, D], F32, kind="ExternalInput")
        self.woT = d("woT", [D, D], F32, kind="ExternalInput")
        self.bqd = d("bqd", [D], F32, kind="ExternalInput")
        self.bkd = d("bkd", [D], F32, kind="ExternalInput")
        self.bvd = d("bvd", [D], F32, kind="ExternalInput")
        self.bod = d("bod", [D], F32, kind="ExternalInput")
        self.WinT = d("WinT", [DFF, D], F32, kind="ExternalInput")
        self.bind = d("bind", [DFF], F32, kind="ExternalInput")
        self.Wout = d("Wout", [DFF, D], F32, kind="ExternalInput")
        self.boutd = d("boutd", [D], F32, kind="ExternalInput")
        self.rbT = d("rbT", [H, 32], F32, kind="ExternalInput")
        self.btab = d("btab", [H, NBAND, 128, 512], F32, kind="ExternalInput")
        self.out = d("out", [LQ, D], F32, kind="ExternalOutput")
        self.hid = d("hid", [NFC * 128 * LQ], F32)
        self.nsp = d("nsp", [NFC * 128 * D], F32)
        self.agin = d("agin", [3 * H * HSZ], F32)
        self.agout = d("agout", [2, 3, H, HSZ], F32)
        self.ag2in = d("ag2in", [HLOC * DH * LK], F32)
        self.ag2out = d("ag2out", [2, HLOC, DH, LK], F32)
        self.arin = d("arin", [4, 1], F32)
        self.arout = d("arout", [4, 1], F32)
        self.msk = d("msk", [128, 2], F32, kind="ExternalInput")
        self.agl = self.agout
        self.ag2l = self.ag2out
        self.taps = dict(taps)
        self.tapd = {}
        for name, shape in self.taps.items():
            self.tapd[name] = d(name, shape, F32, kind="ExternalOutput")

    def emit(self, e, fn, sem=None, inc=1):
        if e == "v":
            own = self.s_v
        elif e == "s":
            own = self.s_s
        else:
            own = None
        if own is not None:
            assert sem is None or sem is own, "v/s ops must milestone their own sem"
            if own.v > 0:
                self.cmd[e].append(lambda E, s=own.h, vv=own.v: E.wait_ge(s, vv))
            own.bump(1)
            self.cmd[e].append(lambda E, fn=fn, s=own.h: fn(E).then_inc(s, 1))
            return
        if sem is not None:
            sem.bump(inc)
            self.cmd[e].append(lambda E, fn=fn, s=sem.h, i=inc: fn(E).then_inc(s, i))
        else:
            self.cmd[e].append(fn)

    def wait(self, e, sem, val=None):
        v = sem.v if val is None else val
        if v > 0:
            self.cmd[e].append(lambda E, s=sem.h, vv=v: E.wait_ge(s, vv))

    def barrier(self):
        b = self.s_bar
        for e in self.cmd:
            self.wait(e, self.s_g)
            self.wait(e, self.s_y)
            self.wait(e, self.s_cc)
            self.wait(e, self.s_t)
            self.wait(e, self.s_v)
            self.wait(e, self.s_s)
            self.wait(e, self.s_gp)
            self.wait(e, self.s_ar)
            for s in self.s_w + self.s_h2 + self.s_p + self.s_gq + self.s_yq:
                self.wait(e, s)
        for _ in range(5):
            b.v += 1
        tgt = b.v
        for e in self.cmd:
            self.cmd[e].append(lambda E, s=b.h: E.nop().then_inc(s, 1))
        for e in self.cmd:
            self.cmd[e].append(lambda E, s=b.h, vv=tgt: E.wait_ge(s, vv))

    def dma(self, q, fn):
        pool = self.s_gq if q == "g" else self.s_yq
        par = self.qpar[q]
        self.qpar[q] ^= 1
        s = pool[par]
        self.wait(q, s)
        self.emit(q, fn, s, 16)
        return (s, s.v)

    def waitd(self, e, tok):
        if tok is not None:
            self.wait(e, tok[0], tok[1])

    def qfence(self, e):
        for s in self.s_gq + self.s_yq:
            self.wait(e, s)

    def col(self, i):
        return self.sm[:, i:i + 1]

    def tap(self, name, ap_fn):
        if name not in self.tapd:
            return
        t = self.tapd[name]
        self.barrier()
        self.dma("g", lambda E, o=t, f=ap_fn: E.dma_start(o.ap(), f()))
        self.barrier()

    def build(self):
        nc = self.nc
        with ExitStack() as stk:
            sem = lambda n: Sem(stk.enter_context(nc.semaphore(n)))
            sb = lambda n, shape, dt=F32: stk.enter_context(nc.sbuf_tensor(n, shape, dt))
            self.s_y = sem("s_y")
            self.s_g = sem("s_g")
            self.s_cc = sem("s_cc")
            self.s_t = sem("s_t")
            self.s_v = sem("s_v")
            self.s_s = sem("s_s")
            self.s_gp = sem("s_gp")
            self.s_bar = sem("s_bar")
            self.s_ar = sem("s_ar")
            self.s_gq = [sem("s_gq0"), sem("s_gq1")]
            self.s_yq = [sem("s_yq0"), sem("s_yq1")]
            self.qpar = {"g": 0, "y": 0}
            self.s_w = [sem("s_w0"), sem("s_w1")]
            self.s_h2 = [sem("s_h0"), sem("s_h1")]
            self.s_p = [sem("s_p0"), sem("s_p1")]
            self.x = sb("x", [128, NTI * D])
            self.ar = sb("arn", [128, 16384])
            self.r1 = sb("r1", [128, 2048])
            self.r2 = sb("r2", [128, 2048])
            self.r3 = sb("r3", [128, 2048])
            self.big = sb("big", [128, 8192])
            self.ones = sb("ones1", [128, 128])
            self.ident = sb("ident", [128, 128])
            self.io_r = sb("io_r", [128, 128])
            self.binv = sb("binv", [128, 32])
            self.bovh = sb("bovh", [128, 8])
            self.bqv = sb("bqv", [128, 8])
            self.bkv = sb("bkv", [128, 8])
            self.bvv = sb("bvv", [128, 8])
            self.bo_v = sb("bo_v", [128, 8])
            self.srv = sb("srv", [128, 32])
            self.gmv = sb("gmv", [128, 32])
            self.nq2 = sb("nq2", [128, 16])
            self.nk2 = sb("nk2", [128, 16])
            self.nq8 = sb("nq8", [128, 8])
            self.nk8 = sb("nk8", [128, 8])
            self.mskv = sb("mskv", [128, 2])
            self.sm = sb("sm", [128, 64])
            self.rbh = sb("rbh", [1, 64])
            self.a32 = sb("a32", [128, 1024])
            self.zb = sb("zb", [128, 512])
            self.ustg = sb("ustg", [128, 512])
            self.idt = sb("idt", [1, 1], I32)
            self.pb = [stk.enter_context(nc.psum_tensor("pb%d" % i, [128, 512], F32))
                       for i in range(8)]

            self.emit_all()

            blk = stk.enter_context(nc.Block())

            @blk.tensor
            def _(E):
                for f in self.cmd["t"]:
                    f(E)

            @blk.vector
            def _(E):
                for f in self.cmd["v"]:
                    f(E)

            @blk.scalar
            def _(E):
                for f in self.cmd["s"]:
                    f(E)

            @blk.gpsimd
            def _(E):
                for f in self.cmd["g"]:
                    f(E)

            @blk.sync
            def _(E):
                for f in self.cmd["y"]:
                    f(E)
        return nc

    def emit_all(self):
        s_y, s_g, s_v, s_t, s_s = self.s_y, self.s_g, self.s_v, self.s_t, self.s_s
        for ti in range(NTI):
            self.dma("g", lambda E, ti=ti: E.dma_start(
                self.x[:, D * ti:D * (ti + 1)], self.xin[128 * ti:128 * (ti + 1), :]))
        self.dma("g", lambda E: E.dma_start(
            self.binv[:, :], bass.AP(self.bind, 0, [[1, 128], [128, 32]]),
            allow_slow_non_contiguous=True))
        for vec, dst in [(self.boutd, self.bovh), (self.bqd, self.bqv),
                         (self.bkd, self.bkv), (self.bvd, self.bvv),
                         (self.bod, self.bo_v)]:
            self.dma("g", lambda E, v=vec, dd=dst: E.dma_start(
                dd[:, :], bass.AP(v, 0, [[1, 128], [128, 8]]),
                allow_slow_non_contiguous=True))
        self.dma("g", lambda E: E.dma_start(self.ar[0:16, 0:32], self.rbT[:, :]))
        idtok = self.dma("g", lambda E: E.dma_start(self.idt[:, :], self.mid[:, :]))
        self.dma("g", lambda E: E.dma_start(self.mskv[:, :], self.msk[:, :]))
        self.emit("g", lambda E: E.memset(self.ones[:, :], 1.0))
        self.emit("g", lambda E: E.memset(self.col(S_EPS), EPS))
        self.emit("g", lambda E: E.iota(
            bass.AP(self.io_r, 0, [[128, 128], [1, 128]]), [[1, 128]],
            channel_multiplier=0, allow_small_or_imprecise_dtypes=True))
        self.emit("g", lambda E: E.iota(
            bass.AP(self.ident, 0, [[128, 128], [1, 128]]), [[0, 128]],
            channel_multiplier=1, allow_small_or_imprecise_dtypes=True), self.s_gp)

        self.qfence("g")
        _idsem, _idval = idtok

        def mkregs(E):
            E.wait_ge(_idsem.h, _idval)
            rk = E.alloc_register(name="rk")
            E.load(rk, self.idt[0:1, 0:1])
            jj = E.alloc_register(name="jj")
            E.reg_mod(jj, rk, 2)
            r0 = E.alloc_register(name="r0")
            E.reg_sub(r0, rk, jj)
            self.regs.update(rk=rk, jj=jj, r0=r0)
        self.cmd["g"].append(mkregs)

        self.qfence("v")
        self.wait("v", self.s_gp)
        self.emit("v", lambda E: E.tensor_tensor(
            self.ident[:, :], self.ident[:, :], self.io_r[:, :], ALU.is_equal))
        self.emit("v", lambda E: E.tensor_scalar(
            self.bovh[:, :], self.bovh[:, :], 0.5, None, ALU.mult))
        self.emit("v", lambda E: E.tensor_scalar(
            self.bkv[:, :], self.bkv[:, :], 0.125, None, ALU.mult))
        self.emit("v", lambda E: E.tensor_reduce(
            self.ar[0:16, 32:33], self.ar[0:16, 0:32], axis=AXX, op=ALU.max), s_v)
        self.wait("t", s_v)
        self.emit("t", lambda E: E.transpose(
            self.pb[7][0:1, 0:16], self.ar[0:16, 32:33], self.ident[0:16, 0:16]), s_t)
        self.wait("v", s_t)
        self.emit("v", lambda E: E.tensor_reduce(
            self.ar[0:1, 33:34], self.pb[7][0:1, 0:16], axis=AXX, op=ALU.max), s_v)
        self.wait("t", s_v)
        self.emit("t", lambda E: E.matmul(
            self.pb[7][:, 0:1], self.ones[0:1, 0:128], self.ar[0:1, 33:34],
            start=True, stop=True), s_t)
        self.wait("v", s_t)
        self.emit("v", lambda E: E.tensor_scalar(
            self.col(S_RB7), self.pb[7][:, 0:1], -1.0, 7.0, ALU.mult, ALU.add), s_v)
        self.barrier()

        self.tap("tap_x0", lambda: self.x[:, 0:16384])
        self.ffn(0)
        self.attn()
        self.ffn(1)

        self.barrier()
        for ti in range(NTI):
            self.dma("y", lambda E, ti=ti: E.dma_start(
                self.out[128 * ti:128 * (ti + 1), :], self.x[:, D * ti:D * (ti + 1)]))
        self.qfence("y")

    def layernorm_to_arena(self, amax=False):
        s_v, s_t, s_s = self.s_v, self.s_t, self.s_s
        if amax:
            self.emit("g", lambda E: E.memset(self.col(S_AMY), 0.0), self.s_gp)
            self.wait("v", self.s_gp)
        guard = {}
        tprev = [0]
        for ti in range(NTI):
            xt = self.x[:, D * ti:D * (ti + 1)]
            self.wait("v", s_t, tprev[0])
            self.emit("v", lambda E, xt=xt: E.tensor_reduce(
                self.col(S_MEAN), xt, axis=AXX, op=ALU.add))
            self.emit("v", lambda E: E.tensor_scalar(
                self.col(S_MEAN), self.col(S_MEAN), 1.0 / D, None, ALU.mult), s_v)
            self.wait("s", s_v)
            self.emit("s", lambda E, xt=xt: E.activation(
                self.r1[:, 0:D], xt, AF.Square), s_s)
            self.wait("v", s_s)
            self.emit("v", lambda E: E.tensor_reduce(
                self.col(S_E2), self.r1[:, 0:D], axis=AXX, op=ALU.add))
            self.emit("v", lambda E: E.tensor_scalar(
                self.col(S_E2), self.col(S_E2), 1.0 / D, None, ALU.mult))
            self.emit("v", lambda E: E.tensor_scalar(
                self.col(S_VAR), self.col(S_MEAN), self.col(S_MEAN), None, ALU.mult))
            self.emit("v", lambda E: E.tensor_tensor(
                self.col(S_VAR), self.col(S_E2), self.col(S_VAR), ALU.subtract), s_v)
            self.wait("s", s_v)
            self.emit("s", lambda E: E.activation(
                self.col(S_RSTD), self.col(S_VAR), AF.Sqrt, bias=self.col(S_EPS)), s_s)
            self.wait("v", s_s)
            self.emit("v", lambda E: E.reciprocal(self.col(S_RSTD2), self.col(S_RSTD)))
            self.emit("v", lambda E: E.tensor_scalar(
                self.col(S_NEGMB), self.col(S_MEAN), self.col(S_RSTD2), -1.0,
                ALU.mult, ALU.mult))
            self.emit("v", lambda E, xt=xt: E.tensor_scalar(
                self.r2[:, 0:D], xt, self.col(S_RSTD2), self.col(S_NEGMB),
                ALU.mult, ALU.add), s_v)
            if amax:
                self.emit("v", lambda E: E.tensor_reduce(
                    self.col(S_SQ), self.r2[:, 0:D], axis=AXX, op=ALU.max,
                    apply_absolute_value=True))
                self.emit("v", lambda E: E.tensor_tensor(
                    self.col(S_AMY), self.col(S_AMY), self.col(S_SQ), ALU.max), s_v)
            yv = s_v.v
            for cc in range(NCC):
                bi = 4 + (cc % 2)
                self.wait("t", s_v, max(yv, guard.get(bi, 0)))
                self.emit("t", lambda E, bi=bi, cc=cc: E.transpose(
                    self.pb[bi][:, 0:128], self.r2[:, 128 * cc:128 * (cc + 1)],
                    self.ident[:, :]), s_t)
                self.wait("v", s_t)
                dst = self.ar[:, 2048 * cc + 128 * ti: 2048 * cc + 128 * (ti + 1)]
                self.emit("v", lambda E, bi=bi, dd=dst: E.tensor_copy(
                    dd, self.pb[bi][:, 0:128]), s_v)
                guard[bi] = s_v.v
            tprev[0] = s_t.v

    def fold_allreduce(self, amax_col, slot, out_col):
        s_v, s_t, s_g = self.s_v, self.s_t, self.s_g
        self.barrier()
        self.emit("t", lambda E, c=amax_col: E.transpose(
            self.pb[7][0:1, 0:128], self.col(c), self.ident[:, :]), s_t)
        self.wait("v", s_t)
        self.emit("v", lambda E: E.tensor_reduce(
            self.sm[0:1, S_AR:S_AR + 1], self.pb[7][0:1, 0:128], axis=AXX,
            op=ALU.max))
        self.emit("v", lambda E: E.tensor_scalar(
            self.sm[0:1, S_AR:S_AR + 1], self.sm[0:1, S_AR:S_AR + 1], 1e-8, None,
            ALU.max), s_v)
        self.wait("g", s_v)
        self.emit("g", lambda E, sl=slot: E.dma_start(
            self.arin[sl:sl + 1, :], self.sm[0:1, S_AR:S_AR + 1]), self.s_ar, 16)
        self.wait("g", self.s_ar)
        self.emit("g", lambda E, sl=slot: E.collective_compute(
            "AllReduce", ALU.max, replica_groups=[list(range(8))],
            ins=[self.arin[sl:sl + 1, :]], outs=[self.arout[sl:sl + 1, :]]),
            self.s_cc)
        self.wait("g", self.s_cc)
        self.emit("g", lambda E, sl=slot: E.dma_start(
            self.sm[0:1, S_AR:S_AR + 1], self.arout[sl:sl + 1, :]), self.s_ar, 16)
        self.wait("t", self.s_ar)
        self.emit("t", lambda E: E.matmul(
            self.pb[7][:, 0:1], self.ones[0:1, 0:128], self.sm[0:1, S_AR:S_AR + 1],
            start=True, stop=True), s_t)
        self.wait("v", s_t)
        self.emit("v", lambda E: E.reciprocal(self.col(S_REC), self.pb[7][:, 0:1]))
        self.emit("v", lambda E, oc=out_col: E.tensor_scalar(
            self.col(oc), self.col(S_REC), 32767.0, None, ALU.mult), s_v)
        self.barrier()

    def round_chain(self, src, dst, scol, s0, width):
        u = self.r1[:, 0:width]
        q = self.r3[:, 0:width]
        self.emit("v", lambda E, s=src, u=u, sc=scol, s0=s0: E.tensor_scalar(
            u, s, sc, s0, ALU.mult, ALU.add))
        self.emit("v", lambda E, u=u, q=q: E.tensor_scalar(
            q, u, C64, -C64, ALU.add, ALU.add))
        self.emit("v", lambda E, u=u, q=q: E.tensor_tensor(u, u, q, ALU.subtract))
        self.emit("v", lambda E, u=u: E.tensor_scalar(
            u, u, C32, -C32, ALU.add, ALU.add))
        self.emit("v", lambda E, u=u: E.scalar_tensor_tensor(
            u, u, 1.0, u, ALU.is_ge, ALU.mult))
        self.emit("v", lambda E, q=q, u=u, dd=dst, s0=s0: E.scalar_tensor_tensor(
            dd, q, -s0, u, ALU.add, ALU.add), self.s_v)

    def ffn(self, idx):
        s_v, s_t, s_s, s_g, s_y = self.s_v, self.s_t, self.s_s, self.s_g, self.s_y
        self.layernorm_to_arena(amax=True)
        self.barrier()
        self.tap("tap_yT%d" % idx, lambda: self.ar[:, 0:16384])
        self.fold_allreduce(S_AMY, 2 * idx, S_SX)
        for cc in range(NCC):
            a = self.ar[:, 2048 * cc:2048 * (cc + 1)]
            self.round_chain(a, a, self.col(S_SX), 40000.0, 2048)
        self.barrier()
        self.tap("tap_M%d" % idx, lambda: self.ar[:, 0:16384])

        self.emit("g", lambda E: E.memset(self.col(S_AMH), 0.0), self.s_gp)
        self.wait("v", self.s_gp)
        wsl = [self.big[:, 0:1024], self.big[:, 1024:2048]]
        ntl = [self.big[:, 2048:3072], self.big[:, 3072:4096]]
        hsl = [self.big[:, 4096:6144], self.big[:, 6144:8192]]
        w_free = [0, 0]
        h_free = [0, 0]
        bank_free = [0, 0, 0, 0]
        gelu_sv = [0, 0]
        tguard = {}
        for ff in range(NFC):
            pp = ff % 2
            gam = S_GAM if pp == 0 else S_GAMB
            self.wait("y", s_t, w_free[pp])
            self.emit("y", lambda E, ff=ff, pp=pp: E.dma_start(
                wsl[pp], self.WinT[128 * ff:128 * (ff + 1), :]), self.s_w[pp], 16)
            self.wait("v", self.s_w[pp])
            self.wait("v", s_s, gelu_sv[pp])
            if ff >= 2:
                self.emit("v", lambda E, pp=pp: E.tensor_reduce(
                    self.col(S_T1), hsl[pp], axis=AXX, op=ALU.max,
                    apply_absolute_value=True))
                self.emit("v", lambda E: E.tensor_tensor(
                    self.col(S_AMH), self.col(S_AMH), self.col(S_T1), ALU.max),
                    s_v)
            w = wsl[pp]
            self.emit("v", lambda E, w=w: E.tensor_reduce(
                self.col(S_WA), w, axis=AXX, op=ALU.max, apply_absolute_value=True))
            self.emit("v", lambda E: E.tensor_scalar(
                self.col(S_WA), self.col(S_WA), 1e-8, None, ALU.max))
            self.emit("v", lambda E: E.reciprocal(self.col(S_SF), self.col(S_WA)))
            self.emit("v", lambda E: E.tensor_scalar(
                self.col(S_SF), self.col(S_SF), 127.0, None, ALU.mult))
            self.emit("v", lambda E: E.tensor_scalar(
                self.col(S_GAM2), self.col(S_SF), self.col(S_SX), None, ALU.mult))
            self.emit("v", lambda E, gam=gam: E.reciprocal(
                self.col(gam), self.col(S_GAM2)), s_v)
            self.round_chain(w, w, self.col(S_SF), 384.0, 1024)
            nq = s_v.v
            for c8 in range(NCC):
                bi = 4 + (c8 % 2)
                self.wait("t", s_v, max(nq, tguard.get(bi, 0)))
                self.emit("t", lambda E, bi=bi, w=w, c8=c8: E.transpose(
                    self.pb[bi][:, 0:128], w[:, 128 * c8:128 * (c8 + 1)],
                    self.ident[:, :]), s_t)
                self.wait("v", s_t)
                self.emit("v", lambda E, bi=bi, c8=c8, pp=pp: E.tensor_copy(
                    ntl[pp][:, 128 * c8:128 * (c8 + 1)], self.pb[bi][:, 0:128]), s_v)
                tguard[bi] = s_v.v
            w_free[pp] = s_t.v
            ntv = s_v.v
            self.wait("t", s_v, ntv)
            for t4 in range(NT4):
                self.wait("t", s_s, bank_free[t4])
            for c8 in range(NCC):
                for t4 in range(NT4):
                    last = (c8 == NCC - 1 and t4 == NT4 - 1)
                    self.emit("t", lambda E, c8=c8, t4=t4, pp=pp: E.matmul(
                        self.pb[t4][:, :], ntl[pp][:, 128 * c8:128 * (c8 + 1)],
                        self.ar[:, 2048 * c8 + 512 * t4: 2048 * c8 + 512 * (t4 + 1)],
                        start=(c8 == 0), stop=(c8 == NCC - 1)),
                        s_t if last else None)
            mmv = s_t.v
            self.wait("s", s_t, mmv)
            self.wait("s", s_v, s_v.v)
            self.wait("s", self.s_p[pp], h_free[pp])
            for t4 in range(NT4):
                self.emit("s", lambda E, t4=t4, pp=pp, ff=ff, gam=gam: E.activation(
                    hsl[pp][:, 512 * t4:512 * (t4 + 1)], self.pb[t4][:, :], AF.Gelu,
                    bias=self.binv[:, ff:ff + 1], scale=self.col(gam)), s_s)
                bank_free[t4] = s_s.v
            sv = s_s.v
            gelu_sv[pp] = sv
            self.wait("g", s_s, sv)
            self.emit("g", lambda E, ff=ff, pp=pp: E.dma_start(
                bass.AP(self.hid, ff * 128 * LQ, [[LQ, 128], [1, LQ]]), hsl[pp]),
                self.s_p[pp], 16)
            h_free[pp] = self.s_p[pp].v
        for pp in (0, 1):
            self.wait("v", s_s, gelu_sv[pp])
            self.emit("v", lambda E, pp=pp: E.tensor_reduce(
                self.col(S_T1), hsl[pp], axis=AXX, op=ALU.max,
                apply_absolute_value=True))
            self.emit("v", lambda E: E.tensor_tensor(
                self.col(S_AMH), self.col(S_AMH), self.col(S_T1), ALU.max), s_v)
        self.barrier()

        self.fold_allreduce(S_AMH, 2 * idx + 1, S_SH)
        wq_free = [None, None]
        for rr in range(NFC):
            pp = rr % 2
            self.waitd("y", wq_free[pp])
            self.emit("y", lambda E, rr=rr, pp=pp: E.dma_start(
                wsl[pp], self.Wout[128 * rr:128 * (rr + 1), :]), self.s_w[pp], 16)
            self.wait("v", self.s_w[pp])
            w = wsl[pp]
            self.emit("v", lambda E, w=w: E.tensor_reduce(
                self.col(S_WA), w, axis=AXX, op=ALU.max, apply_absolute_value=True))
            self.emit("v", lambda E: E.tensor_scalar(
                self.col(S_WA), self.col(S_WA), 1e-8, None, ALU.max))
            self.emit("v", lambda E: E.reciprocal(self.col(S_SF), self.col(S_WA)))
            self.emit("v", lambda E, rr=rr: E.tensor_scalar(
                self.srv[:, rr:rr + 1], self.col(S_SF), 127.0, None, ALU.mult), s_v)
            self.round_chain(w, w, self.srv[:, rr:rr + 1], 384.0, 1024)
            self.wait("g", s_v)
            wq_free[pp] = self.dma("g", lambda E, rr=rr, w=w: E.dma_start(
                bass.AP(self.nsp, rr * 128 * D, [[D, 128], [1, D]]), w))
        self.barrier()
        self.emit("v", lambda E: E.tensor_scalar(
            self.nq2[:, 0:16], self.srv[:, 0:16], self.col(S_SH), None, ALU.mult))
        self.emit("v", lambda E: E.tensor_scalar(
            self.nk2[:, 0:16], self.srv[:, 16:32], self.col(S_SH), None, ALU.mult))
        self.emit("v", lambda E: E.reciprocal(self.gmv[:, 0:16], self.nq2[:, 0:16]))
        self.emit("v", lambda E: E.reciprocal(self.gmv[:, 16:32], self.nk2[:, 0:16]),
                  s_v)
        self.barrier()

        nsl = [self.big[:, 0:1024], self.big[:, 1024:2048]]
        htl = [self.big[:, 2048 + 512 * i:2048 + 512 * (i + 1)] for i in range(2)]
        esl = self.big[:, 4096:8192]
        n_free = [0, 0]
        ht_free = [0, 0]
        m2g, m2gv = 0, 0
        for t4 in range(NT4):
            for rr in range(NFC):
                pp = rr % 2
                self.wait("y", s_t, n_free[pp])
                self.emit("y", lambda E, rr=rr, pp=pp: E.dma_start(
                    nsl[pp], bass.AP(self.nsp, rr * 128 * D, [[D, 128], [1, D]])),
                    self.s_w[pp], 16)
                nready = self.s_w[pp].v
                self.wait("g", s_t, ht_free[pp])
                self.emit("g", lambda E, rr=rr, t4=t4, pp=pp: E.dma_start(
                    htl[pp], bass.AP(self.hid, rr * 128 * LQ + 512 * t4,
                                     [[LQ, 128], [1, 512]])), self.s_h2[pp], 16)
                self.wait("v", self.s_h2[pp])
                self.round_chain(htl[pp], htl[pp], self.col(S_SH), 40000.0, 512)
                self.emit("v", lambda E, pp=pp, rr=rr: E.tensor_scalar(
                    htl[pp], htl[pp], self.gmv[:, rr:rr + 1], None, ALU.mult), s_v)
                self.wait("t", s_v)
                self.wait("t", self.s_w[pp], nready)
                if rr == 0:
                    self.wait("t", s_s, m2g)
                    self.wait("t", s_v, m2gv)
                for o8 in range(8):
                    last = (o8 == 7)
                    self.emit("t", lambda E, o8=o8, pp=pp, rr=rr: E.matmul(
                        self.pb[o8][:, :], nsl[pp][:, 128 * o8:128 * (o8 + 1)],
                        htl[pp], start=(rr == 0), stop=(rr == NFC - 1)),
                        s_t if last else None)
                n_free[pp] = s_t.v
                ht_free[pp] = s_t.v
            mm2 = s_t.v
            self.wait("s", s_t, mm2)
            for o8 in range(8):
                self.emit("s", lambda E, o8=o8: E.activation(
                    esl[:, 512 * o8:512 * (o8 + 1)], self.pb[o8][:, :], AF.Identity,
                    bias=self.bovh[:, o8:o8 + 1], scale=0.5), s_s)
            m2g = s_s.v
            ev = s_s.v
            guard = {}
            for o8 in range(8):
                for j4 in range(4):
                    bi = o8 % 2
                    self.wait("t", s_s, ev)
                    self.wait("t", s_v, guard.get(bi, 0))
                    self.emit("t", lambda E, bi=bi, o8=o8, j4=j4: E.transpose(
                        self.pb[bi][:, 0:128],
                        esl[:, 512 * o8 + 128 * j4: 512 * o8 + 128 * (j4 + 1)],
                        self.ident[:, :]), s_t)
                    self.wait("v", s_t)
                    ti = 4 * t4 + j4
                    dst = self.x[:, D * ti + 128 * o8: D * ti + 128 * (o8 + 1)]
                    self.emit("v", lambda E, bi=bi, dd=dst: E.tensor_tensor(
                        dd, dd, self.pb[bi][:, 0:128], ALU.add), s_v)
                    guard[bi] = s_v.v
            m2gv = s_v.v
        self.barrier()
        self.tap("tap_xf%d" % idx, lambda: self.x[:, 0:16384])

    def attn(self):
        s_v, s_t, s_s, s_g, s_y = self.s_v, self.s_t, self.s_s, self.s_g, self.s_y
        self.layernorm_to_arena(amax=False)
        self.barrier()
        self.tap("tap_yTa", lambda: self.ar[:, 0:16384])
        self.emit("g", lambda E: E.memset(self.nq2[:, :], 0.0))
        self.emit("g", lambda E: E.memset(self.nk2[:, :], 0.0), self.s_gp)
        self.wait("v", self.s_gp)
        self.wait("t", self.s_gp)
        wfull = self.big
        for wt, bias, kind in [(self.wqT, self.bqv, "q"), (self.wkT, self.bkv, "k"),
                               (self.wvT, self.bvv, "v")]:
            self.barrier()
            for c8 in range(NCC):
                self.dma("y", lambda E, wt=wt, c8=c8: E.dma_start(
                    wfull[:, 1024 * c8:1024 * (c8 + 1)],
                    wt[128 * c8:128 * (c8 + 1), :]))
            self.qfence("t")
            bank_guard = {}
            spl_guard = {}
            for d8 in range(NCC):
                for t4 in range(NT4):
                    self.wait("t", s_v, bank_guard.get(t4, 0))
                    for c8 in range(NCC):
                        last = (c8 == NCC - 1)
                        self.emit("t", lambda E, d8=d8, t4=t4, c8=c8, last=last:
                                  E.matmul(
                            self.pb[t4][:, :],
                            wfull[:, 1024 * c8 + 128 * d8: 1024 * c8 + 128 * (d8 + 1)],
                            self.ar[:, 2048 * c8 + 512 * t4: 2048 * c8 + 512 * (t4 + 1)],
                            start=(c8 == 0), stop=last), s_t if last else None)
                    mv = s_t.v
                    self.wait("v", s_t, mv)
                    for tk in spl_guard.get(t4, []):
                        self.waitd("v", tk)
                    self.wait("v", s_s, s_s.v)
                    sc = 0.125 if kind == "k" else 1.0
                    self.emit("v", lambda E, t4=t4, d8=d8, b=bias, sc=sc: E.tensor_scalar(
                        self.r1[:, 512 * t4:512 * (t4 + 1)], self.pb[t4][:, :],
                        sc, b[:, d8:d8 + 1], ALU.mult, ALU.add), s_v)
                    bank_guard[t4] = s_v.v
                    ev = s_v.v
                    if kind in ("q", "k"):
                        self.wait("s", s_v, ev)
                        self.emit("s", lambda E, t4=t4: E.activation(
                            self.r2[:, 512 * t4:512 * (t4 + 1)],
                            self.r1[:, 512 * t4:512 * (t4 + 1)], AF.Square), s_s)
                        self.wait("t", s_s)
                        for hh in range(2):
                            self.wait("t", s_v, bank_guard.get(5, 0))
                            self.emit("t", lambda E, hh=hh, t4=t4: E.matmul(
                                self.pb[5][:, :],
                                self.ones[64 * hh:64 * (hh + 1), 0:128],
                                self.r2[64 * hh:64 * (hh + 1),
                                        512 * t4:512 * (t4 + 1)],
                                start=True, stop=True), s_t)
                            self.wait("v", s_t)
                            dst = self.nq2 if kind == "q" else self.nk2
                            hcol = 2 * d8 + hh
                            self.emit("v", lambda E: E.tensor_reduce(
                                self.col(S_T2), self.pb[5][:, :], axis=AXX,
                                op=ALU.max))
                            self.emit("v", lambda E, dd=dst, hcol=hcol: E.tensor_tensor(
                                dd[:, hcol:hcol + 1], dd[:, hcol:hcol + 1],
                                self.col(S_T2), ALU.max), s_v)
                            bank_guard[5] = s_v.v
                        self.wait("g", s_v, ev)
                        kk = 0 if kind == "k" else 1
                        toks = []
                        for hh in range(2):
                            h = 2 * d8 + hh
                            off = kk * H * HSZ + h * HSZ + 512 * t4
                            toks.append(self.dma(
                                "g", lambda E, off=off, hh=hh, t4=t4: E.dma_start(
                                    bass.AP(self.agin, off, [[LQ, 64], [1, 512]]),
                                    self.r1[64 * hh:64 * (hh + 1),
                                            512 * t4:512 * (t4 + 1)])))
                        spl_guard[t4] = toks
                    else:
                        for j4 in range(4):
                            tc = 4 * t4 + j4
                            slot = tc % 16
                            self.wait("t", s_v, max(ev, bank_guard.get(6, 0)))
                            self.emit("t", lambda E, t4=t4, j4=j4: E.transpose(
                                self.pb[6][:, 0:128],
                                self.r1[:, 512 * t4 + 128 * j4:
                                        512 * t4 + 128 * (j4 + 1)],
                                self.ident[:, :]), s_t)
                            self.wait("v", s_t)
                            for tk in spl_guard.get(300 + slot, []):
                                self.waitd("v", tk)
                            self.emit("v", lambda E, sl=slot: E.tensor_copy(
                                self.r3[:, 128 * sl:128 * (sl + 1)],
                                self.pb[6][:, 0:128]), s_v)
                            bank_guard[6] = s_v.v
                            self.wait("g", s_v)
                            toks = []
                            for hh in range(2):
                                h = 2 * d8 + hh
                                off = 2 * H * HSZ + h * HSZ + 64 * tc
                                toks.append(self.dma(
                                    "g", lambda E, off=off, sl=slot, hh=hh:
                                    E.dma_start(
                                        bass.AP(self.agin, off,
                                                [[1024, 128], [1, 64]]),
                                        self.r3[:, 128 * sl + 64 * hh:
                                                128 * sl + 64 * (hh + 1)])))
                            spl_guard[300 + slot] = toks
        self.barrier()
        self.emit("g", lambda E: E.collective_compute(
            "AllGather", ALU.bypass,
            replica_groups=[[2 * b, 2 * b + 1] for b in range(4)],
            ins=[self.agin.ap().opt()], outs=[self.agout.ap().opt()]), self.s_cc)
        self.wait("g", self.s_cc)
        self.emit("v", lambda E: E.tensor_scalar(
            self.nq8[:, :], self.nq2[:, 0:8], self.mskv[:, 0:1], None, ALU.mult))
        self.emit("v", lambda E: E.scalar_tensor_tensor(
            self.nq8[:, :], self.nq2[:, 8:16], self.mskv[:, 1:2], self.nq8[:, :],
            ALU.mult, ALU.add))
        self.emit("v", lambda E: E.tensor_scalar(
            self.nk8[:, :], self.nk2[:, 0:8], self.mskv[:, 0:1], None, ALU.mult))
        self.emit("v", lambda E: E.scalar_tensor_tensor(
            self.nk8[:, :], self.nk2[:, 8:16], self.mskv[:, 1:2], self.nk8[:, :],
            ALU.mult, ALU.add), s_v)
        self.barrier()

        self._expg = {}
        for hl in range(HLOC):
            self.attn_head(hl)

        self.barrier()
        self.emit("g", lambda E: E.collective_compute(
            "AllGather", ALU.bypass,
            replica_groups=[[2 * b, 2 * b + 1] for b in range(4)],
            ins=[self.ag2in.ap().opt()], outs=[self.ag2out.ap().opt()]), self.s_cc)
        self.wait("g", self.s_cc)
        self.tap("tap_ag2l", lambda: self.ag2l.ap())
        self.barrier()
        self.emit_wo()
        self.barrier()
        self.tap("tap_xa", lambda: self.x[:, 0:16384])

    def attn_head(self, hl):
        s_v, s_t, s_s, s_g, s_y = self.s_v, self.s_t, self.s_s, self.s_g, self.s_y
        self.barrier()
        kap = self.ar[0:64, 0:4096]
        qap = self.ar[0:64, 4096:8192]

        m0 = self.mskv[:, 0:1]
        m1 = self.mskv[:, 1:2]
        jobs = []
        ag, bt = self.agl, self.btab
        for half in range(2):
            jobs.append((self.ar[0:64, 2048 * half:2048 * (half + 1)],
                         self.r1[0:64, 0:2048],
                         lambda E, h=half, hl=hl: ag[h:h + 1, 0:1, hl:hl + 1, :]
                         .squeeze(0).squeeze(0).squeeze(0)
                         .rearrange("(a b) -> a b", a=64),
                         lambda E, h=half, hl=hl: ag[h:h + 1, 0:1, 8 + hl:9 + hl, :]
                         .squeeze(0).squeeze(0).squeeze(0)
                         .rearrange("(a b) -> a b", a=64), 64))
        for half in range(2):
            jobs.append((self.ar[0:64, 4096 + 2048 * half:4096 + 2048 * (half + 1)],
                         self.r1[0:64, 0:2048],
                         lambda E, h=half, hl=hl: ag[h:h + 1, 1:2, hl:hl + 1, :]
                         .squeeze(0).squeeze(0).squeeze(0)
                         .rearrange("(a b) -> a b", a=64),
                         lambda E, h=half, hl=hl: ag[h:h + 1, 1:2, 8 + hl:9 + hl, :]
                         .squeeze(0).squeeze(0).squeeze(0)
                         .rearrange("(a b) -> a b", a=64), 64))
        for half in range(2):
            jobs.append((self.ar[:, 8192 + 1024 * half:8192 + 1024 * (half + 1)],
                         self.r1[:, 0:1024],
                         lambda E, h=half, hl=hl: ag[h:h + 1, 2:3, hl:hl + 1, :]
                         .squeeze(0).squeeze(0).squeeze(0)
                         .rearrange("(a b) -> a b", a=128),
                         lambda E, h=half, hl=hl: ag[h:h + 1, 2:3, 8 + hl:9 + hl, :]
                         .squeeze(0).squeeze(0).squeeze(0)
                         .rearrange("(a b) -> a b", a=128), 128))
        jobs.append((self.ar[:, 10240:12288], self.r2[:, 0:2048],
                     lambda E, hl=hl: bt[hl:hl + 1, :, :, :].squeeze(0)
                     .rearrange("s p q -> p (s q)")[:, 0:2048] if False else None,
                     None, 128))
        ldg = [0]

        def band_src(gidx, s0, s1):
            a = bt[gidx:gidx + 1, s0:s1, :, :].squeeze(0)
            return a.transpose([1, 0, 2])
        jobs = jobs[:-1]

        vguard = s_v.v
        for dst, c1buf, f0, f1, rows in jobs:
            self.wait("g", s_v, vguard)
            t0 = self.dma("g", lambda E, d=dst, f=f0: E.dma_start(d, f(E)))
            t1 = self.dma("g", lambda E, c=c1buf, f=f1: E.dma_start(c, f(E)))
            self.waitd("v", t0)
            self.waitd("v", t1)
            mm0 = m0[0:rows, :]
            mm1 = m1[0:rows, :]
            self.emit("v", lambda E, d=dst, mm0=mm0: E.tensor_scalar(
                d, d, mm0, None, ALU.mult))
            self.emit("v", lambda E, d=dst, c=c1buf, mm1=mm1: E.scalar_tensor_tensor(
                d, c, mm1, d, ALU.mult, ALU.add), s_v)
            vguard = s_v.v
        for s0, s1, c1buf in ((0, 4, self.r2[:, 0:2048]),
                              (4, 5, self.big[:, 2560:3072])):
            ns = s1 - s0
            dst3 = bass.AP(self.ar, 10240 + 512 * s0,
                           [[16384, 128], [512, ns], [1, 512]])
            c1b3 = (bass.AP(self.r2, 0, [[2048, 128], [512, ns], [1, 512]])
                    if s0 == 0 else
                    bass.AP(self.big, 2560, [[8192, 128], [512, ns], [1, 512]]))
            self.wait("g", s_v, vguard)
            t0 = self.dma("g", lambda E, d=dst3, s0=s0, s1=s1, hl=hl: E.dma_start(
                d, band_src(hl, s0, s1)))
            t1 = self.dma("g", lambda E, c=c1b3, s0=s0, s1=s1, hl=hl: E.dma_start(
                c, band_src(8 + hl, s0, s1)))
            self.waitd("v", t0)
            self.waitd("v", t1)
            dst = self.ar[:, 10240 + 512 * s0:10240 + 512 * s1]
            c1buf2 = c1buf[:, 0:512 * ns] if s0 == 0 else c1buf
            self.emit("v", lambda E, d=dst: E.tensor_scalar(
                d, d, m0, None, ALU.mult))
            self.emit("v", lambda E, d=dst, c=c1buf2: E.scalar_tensor_tensor(
                d, c, m1, d, ALU.mult, ALU.add), s_v)
            vguard = s_v.v
        t0 = self.dma("g", lambda E, hl=hl: E.dma_start(
            self.rbh[0:1, 0:32], self.rbT[hl:hl + 1, :]))
        t1 = self.dma("g", lambda E, hl=hl: E.dma_start(
            self.rbh[0:1, 32:64], self.rbT[8 + hl:9 + hl, :]))
        self.waitd("v", t0)
        self.waitd("v", t1)
        self.emit("v", lambda E: E.tensor_scalar(
            self.rbh[0:1, 0:32], self.rbh[0:1, 0:32], m0[0:1, :], None, ALU.mult))
        self.emit("v", lambda E: E.scalar_tensor_tensor(
            self.rbh[0:1, 0:32], self.rbh[0:1, 32:64], m1[0:1, :],
            self.rbh[0:1, 0:32], ALU.mult, ALU.add), s_v)
        self.wait("g", s_v)
        self.emit("g", lambda E: E.memset(
            bass.AP(self.big, 64, [[8192, 128], [65, 32], [1, 1]]), 1.0), self.s_gp)
        self.emit("v", lambda E: E.tensor_copy(
            bass.AP(self.big, 0, [[8192, 128], [65, 32], [1, 64]]),
            bass.AP(self.ar, 8192, [[16384, 128], [64, 32], [1, 64]])), s_v)
        self.emit("v", lambda E, hl=hl: E.tensor_scalar(
            self.col(S_T1), self.nq8[:, hl:hl + 1], self.nk8[:, hl:hl + 1], None,
            ALU.mult), s_v)
        self.wait("s", s_v)
        self.emit("s", lambda E: E.activation(
            self.col(S_T1), self.col(S_T1), AF.Sqrt), s_s)
        self.wait("v", s_s)
        self.emit("v", lambda E: E.scalar_tensor_tensor(
            self.col(S_NCB), self.col(S_T1), -1.0, self.col(S_RB7), ALU.mult,
            ALU.add), s_v)
        self.wait("t", s_v)
        self.emit("t", lambda E: E.matmul(
            self.pb[7][:, 0:2], self.ones[0:1, 0:128],
            bass.AP(self.rbh, 0, [[64, 1], [31, 2]]), start=True, stop=True), s_t)
        self.wait("v", s_t)
        self.emit("v", lambda E: E.tensor_tensor(
            self.col(S_NC0), self.col(S_NCB), self.pb[7][:, 0:1], ALU.add))
        self.emit("v", lambda E: E.tensor_tensor(
            self.col(S_NC31), self.col(S_NCB), self.pb[7][:, 1:2], ALU.add), s_v)
        prep = s_v.v
        self.wait("t", s_v, prep)
        self.wait("t", self.s_gp)
        self.wait("s", s_v, prep)

        aguard = [0, 0]
        pend = None
        for tq in range(NQC):
            for kc in range(NKC):
                sb_ = kc % 2
                asl = kc % 2
                self.wait("t", s_s, self._expg.get(sb_, 0))
                self.emit("t", lambda E, kc=kc, tq=tq, sb_=sb_: E.matmul(
                    self.pb[sb_][:, :], kap[:, 128 * kc:128 * (kc + 1)],
                    qap[:, 512 * tq:512 * (tq + 1)], start=True, stop=True), s_t)
                smm = s_t.v
                if pend is not None:
                    pex, pkc, pasl = pend
                    pend = None
                    self.wait("t", s_s, pex)
                    self.emit("t", lambda E, kc=pkc, asl=pasl: E.matmul(
                        self.pb[2][0:65, :], self.big[:, 65 * kc:65 * (kc + 1)],
                        self.a32[:, 512 * asl:512 * (asl + 1)],
                        start=(kc == 0), stop=False), s_t)
                    aguard[pasl] = s_t.v
                lo, hi = 4 * tq - 1, 4 * tq + 3
                if lo <= kc <= hi:
                    slot = kc - lo
                    if tq == 0:
                        slot = kc + 1
                    self.wait("v", s_t, smm)
                    self.emit("v", lambda E, sb_=sb_, sl=slot: E.tensor_tensor(
                        self.pb[sb_][:, :], self.pb[sb_][:, :],
                        self.ar[:, 10240 + 512 * sl:10240 + 512 * (sl + 1)],
                        ALU.add), s_v)
                    bias_col = self.col(S_NCB)
                    self.wait("s", s_v)
                elif kc < lo:
                    bias_col = self.col(S_NC31)
                    self.wait("s", s_t, smm)
                else:
                    bias_col = self.col(S_NC0)
                    self.wait("s", s_t, smm)
                self.wait("s", s_t, aguard[asl])
                self.emit("s", lambda E, sb_=sb_, asl=asl, bc=bias_col: E.activation(
                    self.a32[:, 512 * asl:512 * (asl + 1)], self.pb[sb_][:, :],
                    AF.Exp, bias=bc), s_s)
                self._expg[sb_] = s_s.v
                pend = (s_s.v, kc, asl)
            pex, pkc, pasl = pend
            pend = None
            self.wait("t", s_s, pex)
            self.emit("t", lambda E, kc=pkc, asl=pasl: E.matmul(
                self.pb[2][0:65, :], self.big[:, 65 * kc:65 * (kc + 1)],
                self.a32[:, 512 * asl:512 * (asl + 1)],
                start=False, stop=True), s_t)
            aguard[pasl] = s_t.v
            umm = s_t.v
            self.wait("v", s_t, umm)
            self.emit("v", lambda E: E.reciprocal(
                self.zb[0:1, :], self.pb[2][64:65, :]), s_v)
            self.wait("t", s_v)
            self.emit("t", lambda E: E.matmul(
                self.pb[3][0:64, :], self.ones[0:1, 0:64], self.zb[0:1, :],
                start=True, stop=True), s_t)
            zv = s_t.v
            self.wait("s", s_t, zv)
            self.waitd("s", getattr(self, "un_tok", None))
            self.emit("s", lambda E: E.copy(self.zb[0:64, :], self.pb[3][0:64, :]))
            self.emit("s", lambda E: E.copy(self.ustg[0:64, :], self.pb[2][0:64, :]),
                      s_s)
            self.wait("v", s_s)
            self.emit("v", lambda E: E.tensor_tensor(
                self.ustg[0:64, :], self.ustg[0:64, :], self.zb[0:64, :],
                ALU.mult), s_v)
            unv = s_v.v
            self.wait("g", s_v, unv)
            self.un_tok = self.dma("g", lambda E, hl=hl, tq=tq: E.dma_start(
                bass.AP(self.ag2in, hl * DH * LK + 512 * tq, [[LK, 64], [1, 512]]),
                self.ustg[0:64, :]))
            self.wait("t", s_s, s_s.v)

    def emit_wo(self):
        s_v, s_t, s_s, s_g, s_y = self.s_v, self.s_t, self.s_s, self.s_g, self.s_y

        m0 = self.mskv[:, 0:1]
        m1 = self.mskv[:, 1:2]
        vguard = s_v.v
        for g in range(H):
            grp = g // 8
            slot = g % 8
            cc = g // 2
            row0 = 64 * (g % 2)
            dst = self.ar[row0:row0 + 64, 2048 * cc:2048 * (cc + 1)]
            c1b = self.r1[row0:row0 + 64, 0:2048]
            self.wait("g", s_v, vguard)
            t0 = self.dma("g", lambda E, d=dst, grp=grp, slot=slot: E.dma_start(
                d, self.ag2l[grp:grp + 1, slot:slot + 1, :, 0:LQ]
                .squeeze(0).squeeze(0)))
            t1 = self.dma("g", lambda E, c=c1b, grp=grp, slot=slot: E.dma_start(
                c, self.ag2l[grp:grp + 1, slot:slot + 1, :, LQ:2 * LQ]
                .squeeze(0).squeeze(0)))
            self.waitd("v", t0)
            self.waitd("v", t1)
            self.emit("v", lambda E, d=dst, row0=row0: E.tensor_scalar(
                d, d, m0[row0:row0 + 64, :], None, ALU.mult))
            self.emit("v", lambda E, d=dst, c=c1b, row0=row0: E.scalar_tensor_tensor(
                d, c, m1[row0:row0 + 64, :], d, ALU.mult, ALU.add), s_v)
            vguard = s_v.v
        self.wait("y", s_v)
        self.wait("t", s_v)
        for c8 in range(NCC):
            self.dma("y", lambda E, c8=c8: E.dma_start(
                self.big[:, 1024 * c8:1024 * (c8 + 1)],
                self.woT[128 * c8:128 * (c8 + 1), :]))
        self.qfence("t")
        bank_guard = {}
        for d8 in range(NCC):
            for t4 in range(NT4):
                self.wait("t", s_v, bank_guard.get(t4, 0))
                for c8 in range(NCC):
                    last = (c8 == NCC - 1)
                    self.emit("t", lambda E, d8=d8, t4=t4, c8=c8, last=last:
                              E.matmul(
                        self.pb[t4][:, :],
                        self.big[:, 1024 * c8 + 128 * d8:1024 * c8 + 128 * (d8 + 1)],
                        self.ar[:, 2048 * c8 + 512 * t4:2048 * c8 + 512 * (t4 + 1)],
                        start=(c8 == 0), stop=last), s_t if last else None)
                mv = s_t.v
                self.wait("s", s_t, mv)
                self.emit("s", lambda E, t4=t4, d8=d8: E.activation(
                    self.r1[:, 512 * t4:512 * (t4 + 1)], self.pb[t4][:, :],
                    AF.Identity, bias=self.bo_v[:, d8:d8 + 1], scale=1.0), s_s)
                ev = s_s.v
                for j4 in range(4):
                    bi = 6 + (j4 % 2)
                    self.wait("t", s_s, ev)
                    self.wait("t", s_v, bank_guard.get(bi, 0))
                    self.emit("t", lambda E, bi=bi, t4=t4, j4=j4: E.transpose(
                        self.pb[bi][:, 0:128],
                        self.r1[:, 512 * t4 + 128 * j4:512 * t4 + 128 * (j4 + 1)],
                        self.ident[:, :]), s_t)
                    self.wait("v", s_t)
                    ti = 4 * t4 + j4
                    dst = self.x[:, D * ti + 128 * d8:D * ti + 128 * (d8 + 1)]
                    self.emit("v", lambda E, bi=bi, dd=dst: E.tensor_tensor(
                        dd, dd, self.pb[bi][:, 0:128], ALU.add), s_v)
                    bank_guard[bi] = s_v.v
                bank_guard[t4] = s_v.v


_NC_CACHE = {}


def _build(taps=()):
    key = tuple((n, tuple(s)) for n, s in taps)
    if key not in _NC_CACHE:
        _NC_CACHE[key] = Build(taps).build()
    return _NC_CACHE[key]


def _bucket_table():
    d = np.arange(-4095, 4096)
    n = d
    nf = np.maximum(n, 1).astype(np.float32)
    val = (np.log(nf / np.float32(16.0)).astype(np.float32)
           / np.float32(np.log(8.0)) * np.float32(16.0)).astype(np.int32) + 16
    val = np.minimum(val, 31)
    ret = np.where(n < 16, n, val)
    return np.maximum(ret, 0).astype(np.int64)


def _make_btab(rel_bias):
    bk = _bucket_table()
    kkk = np.arange(128)[:, None]
    qqq = np.arange(512)[None, :]
    out = np.empty((H, NBAND, 128, 512), dtype=np.float32)
    for s in range(NBAND):
        D0 = 128 - 128 * s
        idx = bk[(D0 + qqq - kkk) + 4095]
        out[:, s] = np.transpose(rel_bias[idx], (2, 0, 1))
    return np.ascontiguousarray(out)


def kernel(L0_attn_x, L0_attn_wq, L0_attn_bq, L0_attn_wk, L0_attn_bk,
           L0_attn_wv, L0_attn_bv, L0_attn_wo, L0_attn_bo, L0_attn_rel_bias,
           L0_ffn_W_in, L0_ffn_b_in, L0_ffn_W_out, L0_ffn_b_out, _taps=()):
    nc = _build(_taps)
    x = np.ascontiguousarray(np.asarray(L0_attn_x, dtype=np.float32))
    com = {
        "wqT": np.asarray(L0_attn_wq).T,
        "wkT": np.asarray(L0_attn_wk).T,
        "wvT": np.asarray(L0_attn_wv).T,
        "woT": np.asarray(L0_attn_wo).T,
        "bqd": np.asarray(L0_attn_bq), "bkd": np.asarray(L0_attn_bk),
        "bvd": np.asarray(L0_attn_bv), "bod": np.asarray(L0_attn_bo),
        "WinT": np.asarray(L0_ffn_W_in).T,
        "bind": np.asarray(L0_ffn_b_in),
        "Wout": np.asarray(L0_ffn_W_out),
        "boutd": np.asarray(L0_ffn_b_out),
        "rbT": np.asarray(L0_attn_rel_bias).T,
        "btab": _make_btab(np.asarray(L0_attn_rel_bias)),
    }
    com = {k: np.ascontiguousarray(np.asarray(v, dtype=np.float32))
           for k, v in com.items()}
    in_maps = []
    for c in range(8):
        b, j = c // 2, c % 2
        m = dict(com)
        m["mid"] = np.array([[c]], dtype=np.int32)
        mk = np.zeros((128, 2), dtype=np.float32)
        mk[:, j] = 1.0
        m["msk"] = mk
        m["xin"] = np.ascontiguousarray(x[b, 2048 * j:2048 * (j + 1), :])
        in_maps.append(m)
    res = run_bass_kernel_spmd(nc, in_maps, core_ids=list(range(8)))
    full = np.empty((4, 4096, 1024), dtype=np.float32)
    for c in range(8):
        b, j = c // 2, c % 2
        full[b, 2048 * j:2048 * (j + 1), :] = res.results[c]["out"]
    if _taps:
        return full, res
    return full
